# version 03
# baseline (speedup 1.0000x reference)
import contextlib
import numpy as np
import ml_dtypes
import concourse.bass as bass
import concourse.mybir as mybir
from concourse.bass_utils import run_bass_kernel_spmd

F32 = mybir.dt.float32
BF16 = mybir.dt.bfloat16
AF = mybir.ActivationFunctionType
ALU = mybir.AluOpType
AX = mybir.AxisListType

R = 8
TL = 2048
T = R * TL
D = 1024
DC = 8
NST = 32
H = 4096
HC = 32
W = 512
NWIN = T // W
WPC = TL // W
NG = 4
GS = NST // NG
PT = 512
EXT = PT + 4
NT1 = TL // PT
BM0 = 3 * D
CM0 = 3 * D + 32


def build(debug=(), stages=3, n_tiles=NT1, do_ag=True, do_compact=True, p1_sub=99, p3_sub=99):
    nc = bass.Bass("TRN2", target_bir_lowering=False)

    def din(name, shape, dt):
        return nc.dram_tensor(name, shape, dt, kind="ExternalInput").ap()

    def dint(name, shape, dt, shared=False):
        kw = dict(kind="Internal")
        if shared:
            kw["addr_space"] = "Shared"
        return nc.dram_tensor(name, shape, dt, **kw).ap()

    def dout(name, shape, dt):
        return nc.dram_tensor(name, shape, dt, kind="ExternalOutput").ap()

    x_ext = din("x_ext", [D, TL + 4], F32)
    mask3 = din("mask3", [128, 3], F32)
    mtab = din("mtab", [8, 1], F32)
    inWT = din("inWT", [D, 2 * D], BF16)
    in_b_pack = din("in_b_pack", [128, 16], F32)
    cw_pack = din("cw_pack", [128, 32], F32)
    cb_pack = din("cb_pack", [128, 8], F32)
    A_log_full = din("A_log_full", [D, NST], F32)
    bcWT = din("bcWT", [D, 64], BF16)
    bcb = din("bcb", [64, 1], F32)
    dtWT = din("dtWT", [D, D], BF16)
    dtb_pack = din("dtb_pack", [128, 8], F32)
    outWT = din("outWT", [D, D], BF16)
    outb_pack = din("outb_pack", [128, 8], F32)
    g_pack = din("g_pack", [128, 8], F32)
    beta_pack = din("beta_pack", [128, 8], F32)
    w1WT = din("w1WT", [D, H], BF16)
    w2WT = din("w2WT", [D, H], BF16)
    h1WT = din("h1WT", [H, H], BF16)
    h2WT = din("h2WT", [H, H], BF16)
    w3WT = din("w3WT", [H, D], BF16)
    w1s_pack = din("w1s_pack", [128, 32], F32)
    w2s_pack = din("w2s_pack", [128, 32], F32)
    h1s_pack = din("h1s_pack", [128, 32], F32)
    h2s_pack = din("h2s_pack", [128, 32], F32)
    w3s_pack = din("w3s_pack", [128, 8], F32)
    oh64 = din("oh64", [64, 64], F32)

    agA_in = dint("agA_in", [2 * D, TL], F32)
    agA_out = dint("agA_out", [R, 2 * D, TL], F32, shared=True)
    agAb_in = dint("agAb_in", [D, TL], BF16)
    agAb_out = dint("agAb_out", [R, D, TL], BF16, shared=True)
    agB_in = dint("agB_in", [3 * D + 64, TL], F32)
    agB_out = dint("agB_out", [R, 3 * D + 64, TL], F32, shared=True)
    agY_in = dint("agY_in", [128, T + D], BF16)
    agY_out = dint("agY_out", [R, 128, T + D], BF16, shared=True)
    c_dt = dint("c_dt", [128, T], F32)
    c_u = dint("c_u", [128, T], F32)
    c_sz = dint("c_sz", [128, T], F32)
    c_x = dint("c_x", [D, TL], F32)
    c_xn = dint("c_xn", [D, TL], F32)
    c_xnbf = dint("c_xnbf", [D, TL], BF16)
    c_yg = dint("c_yg", [D, TL], BF16)
    internal = {"c_dt": c_dt, "c_u": c_u, "c_sz": c_sz, "c_x": c_x, "c_xn": c_xn,
                "c_xnbf": c_xnbf, "c_yg": c_yg, "agB_in": agB_in, "agY_in": agY_in,
                "agA_in": agA_in}

    out_fm = dout("out_fm", [D, TL], F32)
    oid = dout("oid", [1, 1], F32)
    dbg = [(dout("dbg_" + nm, list(internal[nm].shape), internal[nm].dtype), internal[nm])
           for nm in debug]

    with contextlib.ExitStack() as STK:
        def sb(n, s, d):
            return STK.enter_context(nc.sbuf_tensor(n, s, d))

        GA = sb("GA", [128, DC, EXT], F32)
        GB = sb("GB", [128, DC, EXT], F32)
        GC = sb("GC", [128, DC, EXT], F32)
        GD = sb("GDp", [128, DC * EXT + 16], F32).ap().rearrange(
            "p (pad) -> p pad")[:, 0:DC * EXT].rearrange("p (c e) -> p c e", c=DC) if False else sb("GD", [128, DC, EXT + 4], F32)[:, :, 0:EXT]
        GE = sb("GE", [128, DC, EXT], F32)
        GF = sb("GF", [128, DC, EXT], F32)
        ones_col = sb("ones_col", [128, 1], F32)
        ones_row = sb("ones_row", [1, 128], F32)
        mk = sb("mk", [1, 1], F32)
        s_inb = sb("s_inb", [128, 16], F32)
        s_cw = sb("s_cw", [128, 32], F32)
        s_cb = sb("s_cb", [128, 8], F32)
        s_dtb = sb("s_dtb", [128, 8], F32)
        s_bcb = sb("s_bcb", [64, 1], F32)
        s_outb = sb("s_outb", [128, 8], F32)
        s_g = sb("s_g", [128, 8], F32)
        s_beta = sb("s_beta", [128, 8], F32)
        s_w1s = sb("s_w1s", [128, 32], F32)
        s_w2s = sb("s_w2s", [128, 32], F32)
        s_h1s = sb("s_h1s", [128, 32], F32)
        s_h2s = sb("s_h2s", [128, 32], F32)
        s_w3s = sb("s_w3s", [128, 8], F32)
        s_mask = sb("s_mask", [128, 3], F32)
        s_Alog = sb("s_Alog", [128, NST], F32)
        s_Apos = sb("s_Apos", [128, NST], F32)
        s_Aneg = sb("s_Aneg", [128, NST], F32)
        vcar = sb("vcar", [128, NST], F32)
        carry_t = sb("carry_t", [128, GS], F32)
        stat = sb("stat", [1, 3 * EXT], F32)
        xnbft = sb("xnbft", [128, DC, PT], BF16)
        s_oh = sb("s_oh", [64, 64], F32)

        pb = [STK.enter_context(nc.psum_tensor(f"pb{i}", [128, W], F32)) for i in range(8)]

        SEMS = ["qld", "qst", "sT", "sV", "sS", "fB", "fC", "sG", "gate"]
        sem = {nm: STK.enter_context(nc.semaphore(nm)) for nm in SEMS}
        cnt = {nm: 0 for nm in SEMS}

        block = STK.enter_context(nc.Block())

        def inc(nm, instr, k=1):
            instr.then_inc(sem[nm], k)
            cnt[nm] += k
            return instr

        def w8(eng, nm, val=None):
            eng.wait_ge(sem[nm], cnt[nm] if val is None else int(val))

        def bcd(ap2d):
            return ap2d.unsqueeze(1).broadcast_to([ap2d.shape[0], DC, ap2d.shape[1]])

        def _pre_s(sync):
            for t_, src in [(s_inb, in_b_pack), (s_cw, cw_pack), (s_cb, cb_pack),
                            (s_dtb, dtb_pack), (s_bcb, bcb), (s_outb, outb_pack),
                            (s_g, g_pack), (s_beta, beta_pack), (s_w1s, w1s_pack),
                            (s_w2s, w2s_pack), (s_h1s, h1s_pack), (s_h2s, h2s_pack),
                            (s_w3s, w3s_pack), (s_mask, mask3), (s_oh, oh64)]:
                inc("qld", sync.dma_start(out=t_[:], in_=src), 16)
            k_add = [0]
            for arm in sync.switch_core_id(128):
                ml = arm.logical % 8
                a1 = sync.dma_start(out=s_Alog[:], in_=A_log_full[128 * ml:128 * (ml + 1), :])
                a1.then_inc(sem["qld"], 16)
                a2 = sync.dma_start(out=mk[:], in_=mtab[ml:ml + 1, :])
                a2.then_inc(sem["qld"], 16)
                a3 = sync.dma_start(out=agY_in[:, T:], in_=outWT[128 * ml:128 * (ml + 1), :])
                a3.then_inc(sem["qld"], 16)
                k_add[0] = 48
            cnt["qld"] += k_add[0]
        block.sync(_pre_s)

        def _pre_v(vector):
            vector.memset(ones_col[:], 1.0)
            vector.memset(ones_row[:], 1.0)
            vector.memset(vcar[:], 0.0)
            w8(vector, "qld")
            inc("sV", vector.tensor_scalar(s_Apos[:], s_Alog[:], -5.0, 2.0, ALU.max, ALU.min))
        block.vector(_pre_v)

        def _pre_c(scalar):
            w8(scalar, "sV")
            inc("sS", scalar.activation(s_Apos[:], s_Apos[:], AF.Exp))
        block.scalar(_pre_c)

        def _pre_v2(vector):
            w8(vector, "sS")
            inc("sV", vector.tensor_scalar(s_Aneg[:], s_Apos[:], -1.0, None, ALU.mult))
        block.vector(_pre_v2)

        with contextlib.ExitStack() as P1S:
            def sb1(n, s, d):
                return P1S.enter_context(nc.sbuf_tensor(n, s, d))
            SC1 = sb1("SC1", [128, W], F32)
            SC2 = sb1("SC2", [64, W], F32)
            s_inW = sb1("s_inW", [128, DC, 16, 128], BF16)
            s_dtW = sb1("s_dtW", [128, DC, DC, 128], BF16)
            s_bcW = sb1("s_bcW", [128, DC, 64], BF16)
            xnbfe = sb1("xnbfe", [128, DC, EXT], BF16)

            def _w_s(sync):
                w8(sync, "qld")
                inc("qld", sync.dma_start(
                    out=s_inW[:],
                    in_=inWT.rearrange("(kc p) (mc m) -> p kc mc m", p=128, m=128)), 16)
                inc("qld", sync.dma_start(
                    out=s_dtW[:],
                    in_=dtWT.rearrange("(kc p) (mc m) -> p kc mc m", p=128, m=128)), 16)
                inc("qld", sync.dma_start(
                    out=s_bcW[:], in_=bcWT.rearrange("(kc p) m -> p kc m", p=128)), 16)
            block.sync(_w_s)

            for k in range(n_tiles):
                x_t, tmp_t, xn_t, xc_t, xs_t, dt_t = GA, GB, GC, GD, GE, GF
                c0 = PT * k

                def _s_x(sync, k=k):
                    w8(sync, "gate")
                    inc("qld", sync.dma_start(
                        out=GA[:],
                        in_=x_ext.rearrange("(c p) t -> p c t", p=128)[:, :, PT * k: PT * k + EXT]), 16)
                    w8(sync, "qld")
                if p1_sub >= 1:
                    block.sync(_s_x)

                def _c_sq(scalar):
                    w8(scalar, "qld")
                    scalar.drain()
                    inc("sS", scalar.activation(GB[:], GA[:], AF.Square))
                if p1_sub >= 2:
                    block.scalar(_c_sq)

                def _t_stats(tensor):
                    w8(tensor, "sS")
                    for c in range(DC):
                        fl, ll = (c == 0), (c == DC - 1)
                        tensor.matmul(pb[4][0:1, 0:512], ones_col[:], GA[:, c, 0:512], start=fl, stop=ll)
                        tensor.matmul(pb[5][0:1, 0:4], ones_col[:], GA[:, c, 512:516], start=fl, stop=ll)
                        tensor.matmul(pb[6][0:1, 0:512], ones_col[:], GB[:, c, 0:512], start=fl, stop=ll)
                        mm = tensor.matmul(pb[7][0:1, 0:4], ones_col[:], GB[:, c, 512:516], start=fl, stop=ll)
                    inc("sT", mm)
                if p1_sub >= 3:
                    block.tensor(_t_stats)

                mean_v = stat[0:1, 0:EXT]
                ex2_v = stat[0:1, EXT:2 * EXT]
                isg_v = stat[0:1, EXT:2 * EXT]
                nmis_v = stat[0:1, 2 * EXT:3 * EXT]

                def _v_stats(vector):
                    w8(vector, "sT")
                    vector.tensor_scalar(mean_v[:, 0:512], pb[4][0:1, 0:512], 1.0 / D, None, ALU.mult)
                    vector.tensor_scalar(mean_v[:, 512:516], pb[5][0:1, 0:4], 1.0 / D, None, ALU.mult)
                    vector.tensor_scalar(ex2_v[:, 0:512], pb[6][0:1, 0:512], 1.0 / D, None, ALU.mult)
                    vector.tensor_scalar(ex2_v[:, 512:516], pb[7][0:1, 0:4], 1.0 / D, None, ALU.mult)
                    vector.drain()
                    vector.tensor_tensor(nmis_v, mean_v, mean_v, ALU.mult)
                    vector.drain()
                    vector.tensor_tensor(ex2_v, ex2_v, nmis_v, ALU.subtract)
                    vector.drain()
                    inc("sV", vector.tensor_scalar(ex2_v, ex2_v, 1e-5, None, ALU.add))
                if p1_sub >= 4:
                    block.vector(_v_stats)

                def _c_isg(scalar):
                    w8(scalar, "sV")
                    scalar.drain()
                    inc("sS", scalar.activation(isg_v, ex2_v, AF.Ln))
                if p1_sub >= 5:
                    block.scalar(_c_isg)

                def _v_half(vector):
                    w8(vector, "sS")
                    vector.drain()
                    inc("sV", vector.tensor_scalar(isg_v, isg_v, -0.5, None, ALU.mult))
                if p1_sub >= 5:
                    block.vector(_v_half)

                def _c_isg2(scalar):
                    w8(scalar, "sV")
                    scalar.drain()
                    inc("sS", scalar.activation(isg_v, isg_v, AF.Exp))
                if p1_sub >= 5:
                    block.scalar(_c_isg2)

                def _v_nmis(vector):
                    w8(vector, "sS")
                    vector.drain()
                    inc("sV", vector.scalar_tensor_tensor(nmis_v, mean_v, -1.0, isg_v,
                                                          ALU.mult, ALU.mult))
                if p1_sub >= 6:
                    block.vector(_v_nmis)

                def _t_bc(tensor):
                    w8(tensor, "sV")
                    tensor.matmul(pb[4][:, 0:512], ones_row[:], isg_v[:, 0:512], start=True, stop=True)
                    tensor.matmul(pb[5][:, 0:4], ones_row[:], isg_v[:, 512:516], start=True, stop=True)
                    tensor.matmul(pb[6][:, 0:512], ones_row[:], nmis_v[:, 0:512], start=True, stop=True)
                    inc("sT", tensor.matmul(pb[7][:, 0:4], ones_row[:], nmis_v[:, 512:516],
                                            start=True, stop=True))
                if p1_sub >= 7:
                    block.tensor(_t_bc)

                def _v_xn(vector):
                    w8(vector, "sT")
                    vector.tensor_tensor(GC[:, :, 0:512], GA[:, :, 0:512], bcd(pb[4][:, 0:512]), ALU.mult)
                    vector.tensor_tensor(GC[:, :, 512:516], GA[:, :, 512:516], bcd(pb[5][:, 0:4]), ALU.mult)
                    vector.drain()
                    vector.tensor_tensor(GC[:, :, 0:512], GC[:, :, 0:512], bcd(pb[6][:, 0:512]), ALU.add)
                    vector.tensor_tensor(GC[:, :, 512:516], GC[:, :, 512:516], bcd(pb[7][:, 0:4]), ALU.add)
                    vector.drain()
                    vector.tensor_copy(xnbfe[:], GC[:])
                    inc("sV", vector.tensor_copy(xnbft[:], GC[:, :, 2:514]))
                if p1_sub >= 8:
                    block.vector(_v_xn)

                def _s_stA(sync, k=k):
                    w8(sync, "sV")
                    cs = slice(c0, c0 + PT)
                    inc("qst", sync.dma_start(
                        out=agA_in[0:D, cs].rearrange("(c p) t -> p c t", p=128),
                        in_=GA[:, :, 2:514]), 16)
                    inc("qst", sync.dma_start(
                        out=agA_in[D:2 * D, cs].rearrange("(c p) t -> p c t", p=128),
                        in_=GC[:, :, 2:514]), 16)
                    inc("qst", sync.dma_start(
                        out=agAb_in[:, cs].rearrange("(c p) t -> p c t", p=128),
                        in_=xnbft[:]), 16)
                if p1_sub >= 9:
                    block.sync(_s_stA)

                t_base = cnt["sT"]
                fB_base = cnt["fB"]

                def _t_ip(tensor):
                    w8(tensor, "sV")
                    for mc in range(16):
                        bank = mc % 4
                        if mc >= 4:
                            tensor.wait_ge(sem["fB"], fB_base + mc - 3)
                        if mc < 8:
                            for kc in range(DC):
                                tensor.matmul(pb[bank][:, 0:512], s_inW[:, kc, mc, :],
                                              xnbfe[:, kc, 0:512], start=(kc == 0), stop=(kc == DC - 1))
                            for kc in range(DC):
                                mm = tensor.matmul(pb[4 + bank][:, 0:16], s_inW[:, kc, mc, :],
                                                   xnbfe[:, kc, 500:516], start=(kc == 0), stop=(kc == DC - 1))
                        else:
                            for kc in range(DC):
                                mm = tensor.matmul(pb[bank][:, 0:512], s_inW[:, kc, mc, :],
                                                   xnbfe[:, kc, 2:514], start=(kc == 0), stop=(kc == DC - 1))
                        inc("sT", mm)
                if p1_sub >= 10:
                    block.tensor(_t_ip)

                def _v_ipe(vector):
                    for mc in range(8):
                        vector.wait_ge(sem["sT"], t_base + mc + 1)
                        vector.tensor_scalar(GD[:, mc, 0:512], pb[mc % 4][:, 0:512],
                                             s_inb[:, mc:mc + 1], None, ALU.add)
                        inc("fB", vector.tensor_scalar(GD[:, mc, 512:516], pb[4 + mc % 4][:, 12:16],
                                                       s_inb[:, mc:mc + 1], None, ALU.add))
                if p1_sub >= 10:
                    block.vector(_v_ipe)

                def _c_z(scalar):
                    scalar.drain()
                    scalar.wait_ge(sem["fB"], fB_base + 8)
                    for mc in range(8, 16):
                        scalar.wait_ge(sem["sT"], t_base + mc + 1)
                        inc("fB", scalar.activation(GB[:, mc - 8, 0:512], pb[mc % 4][:, 0:512],
                                                    AF.Silu, bias=s_inb[:, mc:mc + 1]))
                if p1_sub >= 10:
                    block.scalar(_c_z)

                def _s_stz(sync, k=k):
                    w8(sync, "fB")
                    inc("qst", sync.dma_start(
                        out=agB_in[2 * D:3 * D, c0:c0 + PT].rearrange("(c p) t -> p c t", p=128),
                        in_=GB[:, :, 0:512]), 16)
                if p1_sub >= 11:
                    block.sync(_s_stz)

                def _v_conv(vector, k=k):
                    vector.drain()
                    if k == 0:
                        for c in range(DC):
                            vector.tensor_tensor(GD[:, c, 0:2], GD[:, c, 0:2],
                                                 s_mask[:, 0:2], ALU.mult)
                    if k == NT1 - 1:
                        for c in range(DC):
                            vector.tensor_tensor(GD[:, c, 514:515], GD[:, c, 514:515],
                                                 s_mask[:, 2:3], ALU.mult)
                    if k == 0 or k == NT1 - 1:
                        vector.drain()
                    for c in range(DC):
                        vector.tensor_scalar(GE[:, c, 0:512], GD[:, c, 0:512],
                                             s_cw[:, 4 * c:4 * c + 1], None, ALU.mult)
                        for m in range(1, 4):
                            vector.drain()
                            vector.scalar_tensor_tensor(GE[:, c, 0:512], GD[:, c, m:512 + m],
                                                        s_cw[:, 4 * c + m:4 * c + m + 1],
                                                        GE[:, c, 0:512], ALU.mult, ALU.add)
                    inc("sV", vector.drain())
                if p1_sub >= 12:
                    block.vector(_v_conv)

                def _c_xs(scalar):
                    w8(scalar, "sV")
                    scalar.drain()
                    for c in range(DC):
                        a = scalar.activation(GE[:, c, 0:512], GE[:, c, 0:512], AF.Silu,
                                              bias=s_cb[:, c:c + 1])
                    inc("sS", a)
                if p1_sub >= 13:
                    block.scalar(_c_xs)

                def _v_xsbf(vector):
                    w8(vector, "sS")
                    vector.drain()
                    inc("sV", vector.tensor_copy(xnbfe[:, :, 0:512], GE[:, :, 0:512]))
                if p1_sub >= 14:
                    block.vector(_v_xsbf)

                t2 = cnt["sT"]
                fb2 = cnt["fB"]

                def _t_dt(tensor, fb2=fb2):
                    w8(tensor, "sV")
                    for mc in range(8):
                        if mc < 4:
                            tensor.wait_ge(sem["fB"], fb2)
                        else:
                            tensor.wait_ge(sem["fB"], fb2 + mc - 3)
                        for kc in range(DC):
                            mm = tensor.matmul(pb[mc % 4][:, 0:512], s_dtW[:, kc, mc, :],
                                               xnbfe[:, kc, 0:512], start=(kc == 0), stop=(kc == DC - 1))
                        inc("sT", mm)
                if p1_sub >= 15:
                    block.tensor(_t_dt)

                def _cv_dt(scalar, t2=t2, fb2=fb2):
                    for mc in range(8):
                        scalar.wait_ge(sem["sT"], t2 + mc + 1)
                        if mc >= 1:
                            scalar.wait_ge(sem["fB"], fb2 + mc)
                        scalar.drain()
                        scalar.activation(SC1[:], pb[mc % 4][:, 0:512], AF.Exp,
                                          bias=s_dtb[:, mc:mc + 1])
                        scalar.drain()
                        inc("sS", scalar.activation(SC1[:], SC1[:], AF.Ln, bias=1.0))
                if p1_sub >= 15:
                    block.scalar(_cv_dt)
                s2 = cnt["sS"] - 8

                def _v_dt(vector):
                    vector.wait_ge(sem["fB"], fb2)
                    for mc in range(8):
                        vector.wait_ge(sem["sS"], s2 + mc + 1)
                        inc("fB", vector.tensor_scalar(GF[:, mc, 0:512], SC1[:], 0.001, 0.2,
                                                       ALU.max, ALU.min))
                if p1_sub >= 15:
                    block.vector(_v_dt)

                def _t_bcm(tensor):
                    w8(tensor, "fB")
                    for kc in range(DC):
                        mm = tensor.matmul(pb[4][0:64, 0:512], s_bcW[:, kc, :],
                                           xnbfe[:, kc, 0:512], start=(kc == 0), stop=(kc == DC - 1))
                    inc("sT", mm)
                if p1_sub >= 16:
                    block.tensor(_t_bcm)

                def _v_bcm(vector):
                    w8(vector, "sT")
                    vector.drain()
                    vector.tensor_scalar(SC2[0:32, :], pb[4][0:32, 0:512], s_bcb[0:32, :], -1.0,
                                         ALU.add, ALU.mult)
                    vector.tensor_scalar(SC2[32:64, :], pb[4][32:64, 0:512], s_bcb[32:64, :], None,
                                         ALU.add)
                    w8(vector, "qst")
                    inc("sV", vector.tensor_tensor(GC[:, :, 0:512], GF[:, :, 0:512],
                                                   GE[:, :, 0:512], ALU.mult))
                if p1_sub >= 16:
                    block.vector(_v_bcm)

                def _s_stB(sync, k=k):
                    w8(sync, "sV")
                    w8(sync, "qst")
                    cs = slice(c0, c0 + PT)
                    inc("qst", sync.dma_start(
                        out=agB_in[0:D, cs].rearrange("(c p) t -> p c t", p=128),
                        in_=GF[:, :, 0:512]), 16)
                    inc("qst", sync.dma_start(
                        out=agB_in[D:2 * D, cs].rearrange("(c p) t -> p c t", p=128),
                        in_=GC[:, :, 0:512]), 16)
                    inc("qst", sync.dma_start(out=agB_in[BM0:BM0 + 64, cs], in_=SC2[:]), 16)
                    w8(sync, "qst")
                if p1_sub >= 17:
                    block.sync(_s_stB)

                def _v_gate(vector):
                    w8(vector, "qst")
                    inc("gate", vector.drain())
                if p1_sub >= 18:
                    block.vector(_v_gate)

        def _s_p1end(sync):
            w8(sync, "qld")
            w8(sync, "qst")
        block.sync(_s_p1end)

        def _g_ag(gpsimd):
            w8(gpsimd, "qst")
            w8(gpsimd, "qld")
            if not do_ag:
                return
            cc = gpsimd.collective_compute("AllGather", ALU.bypass, ins=[agA_in],
                                           outs=[agA_out], replica_groups=[list(range(R))])
            inc("sG", cc)
            cc = gpsimd.collective_compute("AllGather", ALU.bypass, ins=[agAb_in],
                                           outs=[agAb_out], replica_groups=[list(range(R))])
            inc("sG", cc)
            cc = gpsimd.collective_compute("AllGather", ALU.bypass, ins=[agB_in],
                                           outs=[agB_out], replica_groups=[list(range(R))])
            inc("sG", cc)
            w8(gpsimd, "sG")
        block.gpsimd(_g_ag)

        def _s_compact(sync):
            w8(sync, "sG")
            if not do_compact:
                return
            k_add = [0]
            for arm in sync.switch_core_id(128):
                ml = arm.logical % 8
                n = 0
                d0 = 128 * ml
                a = sync.dma_start(out=c_x, in_=agA_out[ml, 0:D, :])
                a.then_inc(sem["qld"], 16); n += 16
                a = sync.dma_start(out=c_xn, in_=agA_out[ml, D:2 * D, :])
                a.then_inc(sem["qld"], 16); n += 16
                a = sync.dma_start(out=c_xnbf, in_=agAb_out[ml])
                a.then_inc(sem["qld"], 16); n += 16
                for r in range(R):
                    cs = slice(TL * r, TL * (r + 1))
                    a = sync.dma_start(out=c_dt[:, cs], in_=agB_out[r, d0:d0 + 128, :])
                    a.then_inc(sem["qld"], 16); n += 16
                    a = sync.dma_start(out=c_u[:, cs], in_=agB_out[r, D + d0:D + d0 + 128, :])
                    a.then_inc(sem["qld"], 16); n += 16
                    a = sync.dma_start(out=c_sz[:, cs], in_=agB_out[r, 2 * D + d0:2 * D + d0 + 128, :])
                    a.then_inc(sem["qld"], 16); n += 16
                k_add[0] = n
            cnt["qld"] += k_add[0]
        block.sync(_s_compact)

        if stages >= 2:
            with contextlib.ExitStack() as P2S:
                def sb2(n, s, d):
                    return P2S.enter_context(nc.sbuf_tensor(n, s, d))
                zeros_w = sb2("zeros_w", [128, W], F32)
                lastc = sb2("lastc", [128, 32], F32)
                dtw = [sb2(f"dtw{i}", [128, W], F32) for i in range(2)]
                uw = [sb2(f"uw{i}", [128, W], F32) for i in range(2)]
                szw = [sb2(f"szw{i}", [128, W], F32) for i in range(2)]
                bcw = [sb2(f"bcw{i}", [64, W], F32) for i in range(2)]
                ywin = sb2("ywin", [128, W], F32)
                ygw = [sb2(f"ygw{i}", [128, W], BF16) for i in range(2)]
                dwin = sb2("dwin", [128, W], F32)
                rs_t = sb2("rs_t", [128, W], F32)
                EmP = [GA[:, :, 0:W], GB[:, :, 0:W]]
                EpP = [GC[:, :, 0:W], GD[:, :, 0:W]]
                qg_b = GE[:, :, 0:W]
                NZ_b = GF[:, :, 0:W]
                LO_b = P2A = sb2("P2A", [128, GS, W], F32)
                HI_b = P2B = sb2("P2B", [128, GS, W], F32)
                vv_b = P2C = sb2("P2C", [128, GS, W], F32)

                def _v_z(vector):
                    vector.memset(zeros_w[:], 0.0)
                    vector.drain()
                block.vector(_v_z)

                ld0 = cnt["qld"]
                vwin_done = []
                store_done = []
                group_idx = 0
                sE_marks = []
                vE_marks = []

                for w in range(NWIN):
                    pw = w % 2
                    r = w // WPC
                    wc0 = (w % WPC) * W

                    def _s_ld(sync, w=w, pw=pw, r=r, wc0=wc0):
                        if w == 0:
                            sync.wait_ge(sem["qld"], ld0)
                        if w >= 2:
                            sync.wait_ge(sem["sV"], vwin_done[w - 2])
                        cs = slice(W * w, W * (w + 1))
                        inc("qld", sync.dma_start(out=dtw[pw][:], in_=c_dt[:, cs]), 16)
                        inc("qld", sync.dma_start(out=uw[pw][:], in_=c_u[:, cs]), 16)
                        inc("qld", sync.dma_start(out=szw[pw][:], in_=c_sz[:, cs]), 16)
                        inc("qld", sync.dma_start(
                            out=bcw[pw][:], in_=agB_out[r, BM0:BM0 + 64, wc0:wc0 + W]), 16)
                        w8(sync, "qld")
                    block.sync(_s_ld)

                    def _v_D(vector, w=w, pw=pw):
                        vector.wait_ge(sem["qld"], ld0 + 64 * (w + 1))
                        if w >= 1:
                            vector.wait_ge(sem["qst"], store_done[w - 1])
                        inc("sV", vector.tensor_tensor_scan(dwin[:], dtw[pw][:], zeros_w[:],
                                                            0.0, ALU.add, ALU.add))
                    block.vector(_v_D)
                    d_mark = cnt["sV"]

                    for g in range(NG):
                        gp = group_idx % 2
                        Em, Ep = EmP[gp], EpP[gp]

                        def _c_E(scalar, g=g, gp=gp, d_mark=d_mark, gi=group_idx):
                            scalar.wait_ge(sem["sV"], d_mark)
                            if gi >= 2:
                                scalar.wait_ge(sem["sV"], vE_marks[gi - 2])
                            scalar.drain()
                            for j in range(GS):
                                n = g * GS + j
                                scalar.activation(EmP[gp][:, j, :], dwin[:], AF.Exp,
                                                  scale=s_Apos[:, n:n + 1])
                                a = scalar.activation(EpP[gp][:, j, :], dwin[:], AF.Exp,
                                                      scale=s_Aneg[:, n:n + 1])
                            inc("sS", a)
                        block.scalar(_c_E)
                        sE_marks.append(cnt["sS"])

                        tb0 = cnt["sT"]
                        fb0 = cnt["fB"]
                        fc0 = cnt["fC"]

                        def _t_g(tensor, g=g, w=w, pw=pw, tb0=tb0, fb0=fb0, fc0=fc0):
                            tensor.wait_ge(sem["qld"], ld0 + 64 * (w + 1))
                            def ohc(col):
                                return s_oh[:, col:col + 1].broadcast_to([64, 128])
                            tensor.wait_ge(sem["fB"], fb0)
                            for j in range(4):
                                n = g * GS + j
                                inc("sT", tensor.matmul(pb[j][:, 0:W], ohc(n),
                                                        bcw[pw][:], start=True, stop=True))
                            tensor.wait_ge(sem["fC"], fc0)
                            for j in range(4):
                                n = g * GS + j
                                inc("sT", tensor.matmul(pb[4 + j][:, 0:W], ohc(32 + n),
                                                        bcw[pw][:], start=True, stop=True))
                            for j in range(4, 8):
                                n = g * GS + j
                                tensor.wait_ge(sem["fB"], fb0 + (j - 4) + 1)
                                inc("sT", tensor.matmul(pb[j % 4][:, 0:W], ohc(n),
                                                        bcw[pw][:], start=True, stop=True))
                            for j in range(4, 8):
                                n = g * GS + j
                                tensor.wait_ge(sem["fC"], fc0 + (j - 4) + 1)
                                inc("sT", tensor.matmul(pb[4 + j % 4][:, 0:W], ohc(32 + n),
                                                        bcw[pw][:], start=True, stop=True))
                        block.tensor(_t_g)

                        def _v_g(vector, g=g, gp=gp, pw=pw, w=w, tb0=tb0, gi=group_idx):
                            Em, Ep = EmP[gp], EpP[gp]
                            vector.wait_ge(sem["sS"], sE_marks[gi])
                            for j in range(GS):
                                if j < 4:
                                    vector.wait_ge(sem["sT"], tb0 + j + 1)
                                else:
                                    vector.wait_ge(sem["sT"], tb0 + 8 + (j - 4) + 1)
                                inc("fB", vector.tensor_tensor(qg_b[:, j, :], pb[j % 4][:, 0:W],
                                                               uw[pw][:], ALU.mult))
                            vector.drain()
                            vector.tensor_tensor(qg_b[:], qg_b[:], Em[:], ALU.mult)
                            vector.drain()
                            for j in range(GS):
                                vector.tensor_tensor_scan(NZ_b[:, j, :], qg_b[:, j, :], zeros_w[:],
                                                          0.0, ALU.add, ALU.add)
                            vector.drain()
                            vector.scalar_tensor_tensor(LO_b[:], Em[:], -10.0, NZ_b[:],
                                                        ALU.mult, ALU.add)
                            vector.scalar_tensor_tensor(HI_b[:], Em[:], 10.0, NZ_b[:],
                                                        ALU.mult, ALU.add)
                            vector.drain()
                            for j in range(GS):
                                n = g * GS + j
                                vector.tensor_tensor_scan(vv_b[:, j, :], LO_b[:, j, :], HI_b[:, j, :],
                                                          vcar[:, n:n + 1], ALU.max, ALU.min)
                            vector.drain()
                            vector.tensor_tensor(vv_b[:], vv_b[:], NZ_b[:], ALU.subtract)
                            vector.drain()
                            vector.tensor_tensor(vv_b[:], vv_b[:], Ep[:], ALU.mult)
                            vector.drain()
                            for j in range(GS):
                                n = g * GS + j
                                vector.tensor_copy(vcar[:, n:n + 1], vv_b[:, j, W - 1:W])
                            for j in range(GS):
                                if j < 4:
                                    vector.wait_ge(sem["sT"], tb0 + 4 + j + 1)
                                else:
                                    vector.wait_ge(sem["sT"], tb0 + 12 + (j - 4) + 1)
                                inc("fC", vector.tensor_tensor(LO_b[:, j, :], vv_b[:, j, :],
                                                               pb[4 + j % 4][:, 0:W], ALU.mult))
                            vector.drain()
                            red_in = LO_b[:].rearrange("p n t -> p t n")
                            if g == 0:
                                vector.tensor_reduce(ywin[:], red_in, AX.X, ALU.add)
                            else:
                                vector.tensor_reduce(rs_t[:], red_in, AX.X, ALU.add)
                                vector.drain()
                                vector.tensor_tensor(ywin[:], ywin[:], rs_t[:], ALU.add)
                            inc("sV", vector.drain())
                        block.vector(_v_g)
                        vE_marks.append(cnt["sV"])
                        group_idx += 1

                    def _v_yg(vector, w=w, pw=pw):
                        inc("sV", vector.tensor_tensor(ygw[pw][:], ywin[:], szw[pw][:], ALU.mult))
                    block.vector(_v_yg)

                    vwin_done.append(cnt["sV"])

                    def _s_sty(sync, w=w, pw=pw):
                        w8(sync, "sV")
                        inc("qst", sync.dma_start(out=agY_in[:, W * w:W * (w + 1)], in_=ygw[pw][:]), 16)
                        w8(sync, "qst")
                    block.sync(_s_sty)
                    store_done.append(cnt["qst"])

            def _g_agy(gpsimd):
                w8(gpsimd, "qst")
                cc = gpsimd.collective_compute("AllGather", ALU.bypass, ins=[agY_in],
                                               outs=[agY_out], replica_groups=[list(range(R))])
                inc("sG", cc)
                w8(gpsimd, "sG")
            block.gpsimd(_g_agy)

            def _s_cyg(sync):
                w8(sync, "sG")
                k_add = [0]
                for arm in sync.switch_core_id(128):
                    ml = arm.logical % 8
                    n = 0
                    for kk in range(R):
                        a = sync.dma_start(out=c_yg[128 * kk:128 * (kk + 1), :],
                                           in_=agY_out[kk, :, TL * ml:TL * (ml + 1)])
                        a.then_inc(sem["qld"], 16); n += 16
                    k_add[0] = n
                cnt["qld"] += k_add[0]
            block.sync(_s_cyg)

        if stages >= 3:
            with contextlib.ExitStack() as P3S:
                def sb3(n, s, d):
                    return P3S.enter_context(nc.sbuf_tensor(n, s, d))
                wdb = [sb3(f"wdb{i}", [128, HC, 128], BF16) for i in range(2)]
                ht = sb3("ht", [128, HC, PT], BF16)
                gt = sb3("gt", [128, HC, PT], BF16)
                ygt = sb3("ygt", [128, DC, PT], BF16)
                silb = sb3("silb", [128, PT], F32)

                for k in range(NT1):
                    c0 = PT * k
                    cs = slice(c0, c0 + PT)

                    def _s_ld3(sync, k=k, cs=cs):
                        w8(sync, "gate", cnt["gate"])
                        w8(sync, "qld")
                        inc("qld", sync.dma_start(
                            out=ygt[:], in_=c_yg.rearrange("(c p) t -> p c t", p=128)[:, :, cs]), 16)
                        inc("qld", sync.dma_start(
                            out=GB[:, :, 0:PT], in_=c_xn.rearrange("(c p) t -> p c t", p=128)[:, :, cs]), 16)
                        inc("qld", sync.dma_start(
                            out=GC[:, :, 0:PT], in_=c_x.rearrange("(c p) t -> p c t", p=128)[:, :, cs]), 16)
                        inc("qld", sync.dma_start(
                            out=xnbft[:], in_=c_xnbf.rearrange("(c p) t -> p c t", p=128)[:, :, cs]), 16)
                        inc("qld", sync.dma_start(
                            out=wdb[0][:].rearrange("p (kc mc) m -> p kc mc m", kc=4),
                            in_=agY_out[0:4, :, T:].rearrange("kc p (mc m) -> p kc mc m", m=128)), 16)
                        inc("qld", sync.dma_start(
                            out=wdb[1][:].rearrange("p (kc mc) m -> p kc mc m", kc=4),
                            in_=agY_out[4:8, :, T:].rearrange("kc p (mc m) -> p kc mc m", m=128)), 16)
                        w8(sync, "qld")
                    if p3_sub >= 1:
                        block.sync(_s_ld3)

                    t3 = cnt["sT"]
                    fb3 = cnt["fB"]
                    NMC = 8

                    def _t_op(tensor, t3=t3, fb3=fb3):
                        w8(tensor, "qld")
                        for mc in range(NMC):
                            if mc >= 4:
                                tensor.wait_ge(sem["fB"], fb3 + mc - 3)
                            for kc in range(DC):
                                wsl = wdb[kc // 4][:, (kc % 4) * 8 + mc, :]
                                tensor.matmul(pb[mc % 4][:, 0:512], wsl, ygt[:, kc, :],
                                              start=(kc == 0), stop=(kc == DC - 1))
                            inc("sT", tensor.drain())
                    if p3_sub >= 2:
                        block.tensor(_t_op)

                    def _v_ope(vector, t3=t3):
                        for mc in range(NMC):
                            vector.wait_ge(sem["sT"], t3 + mc + 1)
                            inc("fB", vector.scalar_tensor_tensor(
                                GA[:, mc, 0:512], pb[mc % 4][:, 0:512], s_outb[:, mc:mc + 1],
                                GB[:, mc, 0:PT], ALU.add, ALU.add))
                    if p3_sub >= 2:
                        block.vector(_v_ope)

                    def _c_sq3(scalar):
                        w8(scalar, "fB")
                        scalar.drain()
                        inc("sS", scalar.activation(GD[:, :, 0:PT], GA[:, :, 0:PT], AF.Square))
                    if p3_sub >= 3:
                        block.scalar(_c_sq3)

                    def _t_st3(tensor):
                        w8(tensor, "sS")
                        for c in range(DC):
                            fl, ll = (c == 0), (c == DC - 1)
                            tensor.matmul(pb[4][0:1, 0:512], ones_col[:], GA[:, c, 0:PT], start=fl, stop=ll)
                            tensor.matmul(pb[6][0:1, 0:512], ones_col[:], GD[:, c, 0:PT], start=fl, stop=ll)
                        inc("sT", tensor.drain())
                    if p3_sub >= 3:
                        block.tensor(_t_st3)

                    mean3 = stat[0:1, 0:PT]
                    ex23 = stat[0:1, EXT:EXT + PT]
                    isg3 = stat[0:1, EXT:EXT + PT]
                    nmis3 = stat[0:1, 2 * EXT:2 * EXT + PT]

                    def _v_st3(vector):
                        w8(vector, "sT")
                        vector.drain()
                        vector.tensor_scalar(mean3, pb[4][0:1, 0:512], 1.0 / D, None, ALU.mult)
                        vector.tensor_scalar(ex23, pb[6][0:1, 0:512], 1.0 / D, None, ALU.mult)
                        vector.drain()
                        vector.tensor_tensor(nmis3, mean3, mean3, ALU.mult)
                        vector.drain()
                        vector.tensor_tensor(ex23, ex23, nmis3, ALU.subtract)
                        vector.drain()
                        inc("sV", vector.tensor_scalar(ex23, ex23, 1e-5, None, ALU.add))
                    if p3_sub >= 4:
                        block.vector(_v_st3)

                    def _c_isg3(scalar):
                        w8(scalar, "sV")
                        scalar.drain()
                        inc("sS", scalar.activation(isg3, ex23, AF.Ln))
                    if p3_sub >= 4:
                        block.scalar(_c_isg3)

                    def _v_h3(vector):
                        w8(vector, "sS")
                        vector.drain()
                        inc("sV", vector.tensor_scalar(isg3, isg3, -0.5, None, ALU.mult))
                    if p3_sub >= 4:
                        block.vector(_v_h3)

                    def _c_isg3b(scalar):
                        w8(scalar, "sV")
                        scalar.drain()
                        inc("sS", scalar.activation(isg3, isg3, AF.Exp))
                    if p3_sub >= 4:
                        block.scalar(_c_isg3b)

                    def _v_nm3(vector):
                        w8(vector, "sS")
                        vector.drain()
                        inc("sV", vector.scalar_tensor_tensor(nmis3, mean3, -1.0, isg3,
                                                              ALU.mult, ALU.mult))
                    if p3_sub >= 4:
                        block.vector(_v_nm3)

                    def _t_bc3(tensor):
                        w8(tensor, "sV")
                        tensor.matmul(pb[5][:, 0:512], ones_row[:], isg3, start=True, stop=True)
                        tensor.matmul(pb[7][:, 0:512], ones_row[:], nmis3, start=True, stop=True)
                        inc("sT", tensor.drain())
                    if p3_sub >= 4:
                        block.tensor(_t_bc3)

                    def _v_ln2(vector):
                        w8(vector, "sT")
                        vector.drain()
                        vector.tensor_tensor(GE[:, :, 0:PT], GA[:, :, 0:PT], bcd(pb[5][:, 0:512]), ALU.mult)
                        vector.drain()
                        inc("sV", vector.tensor_tensor(GE[:, :, 0:PT], GE[:, :, 0:PT],
                                                       bcd(pb[7][:, 0:512]), ALU.add))
                    if p3_sub >= 5:
                        block.vector(_v_ln2)

                    def _v_aff(vector):
                        vector.drain()
                        for c in range(DC):
                            vector.tensor_scalar(GD[:, c, 0:PT], GE[:, c, 0:PT],
                                                 s_g[:, c:c + 1], s_beta[:, c:c + 1],
                                                 ALU.mult, ALU.add)
                        vector.drain()
                        inc("sV", vector.tensor_tensor(GF[:, :, 0:PT], GC[:, :, 0:PT],
                                                       GD[:, :, 0:PT], ALU.add))
                    if p3_sub >= 5:
                        block.vector(_v_aff)

                    fc_base = cnt["fC"]
                    qld_base = cnt["qld"]

                    def _s_w12b(sync, fc_base=fc_base):
                        w8(sync, "sT")
                        for mc in range(HC):
                            if mc >= 2:
                                sync.wait_ge(sem["fC"], fc_base + mc - 1)
                            inc("qld", sync.dma_start(
                                out=wdb[mc % 2][:, 0:8, :],
                                in_=w1WT.rearrange("(kc p) (mc m) -> p kc mc m", p=128, m=128)[:, :, mc, :]), 16)
                            inc("qld", sync.dma_start(
                                out=wdb[mc % 2][:, 8:16, :],
                                in_=w2WT.rearrange("(kc p) (mc m) -> p kc mc m", p=128, m=128)[:, :, mc, :]), 16)
                            w8(sync, "qld")
                    if p3_sub >= 6:
                        block.sync(_s_w12b)

                    t4 = cnt["sT"]

                    def _t_w12(tensor, qld_base=qld_base, fc_base=fc_base):
                        for mc in range(HC):
                            tensor.wait_ge(sem["qld"], qld_base + 32 * (mc + 1))
                            if mc >= 2:
                                tensor.wait_ge(sem["fC"], fc_base + mc - 1)
                            ba = 2 * (mc % 2)
                            for kc in range(DC):
                                tensor.matmul(pb[ba][:, 0:512], wdb[mc % 2][:, kc, :],
                                              xnbft[:, kc, :], start=(kc == 0), stop=(kc == DC - 1))
                            inc("sT", tensor.drain())
                            for kc in range(DC):
                                tensor.matmul(pb[ba + 1][:, 0:512], wdb[mc % 2][:, 8 + kc, :],
                                              xnbft[:, kc, :], start=(kc == 0), stop=(kc == DC - 1))
                            inc("sT", tensor.drain())
                    if p3_sub >= 6:
                        block.tensor(_t_w12)

                    s4 = cnt["sS"]
                    fcw = cnt["fC"]

                    def _c_w1(scalar, t4=t4, fcw=fcw):
                        for mc in range(HC):
                            scalar.wait_ge(sem["sT"], t4 + 2 * mc + 1)
                            if mc >= 1:
                                scalar.wait_ge(sem["fC"], fcw + mc)
                            scalar.drain()
                            inc("sS", scalar.activation(silb[:], pb[2 * (mc % 2)][:, 0:512], AF.Silu,
                                                        scale=s_w1s[:, mc:mc + 1]))
                    if p3_sub >= 6:
                        block.scalar(_c_w1)

                    def _v_w2(vector, t4=t4, s4=s4):
                        for mc in range(HC):
                            vector.wait_ge(sem["sT"], t4 + 2 * mc + 2)
                            vector.wait_ge(sem["sS"], s4 + mc + 1)
                            inc("fC", vector.scalar_tensor_tensor(
                                ht[:, mc, :], pb[2 * (mc % 2) + 1][:, 0:512], s_w2s[:, mc:mc + 1],
                                silb[:], ALU.mult, ALU.mult))
                    if p3_sub >= 6:
                        block.vector(_v_w2)

                    def ffn_big(name, wsrc, rhs_buf, out_buf, s_scale, fc_name="fC"):
                        fcb = cnt[fc_name]
                        qb = cnt["qld"]

                        tprev = cnt["sT"]

                        def _s_w(sync):
                            sync.wait_ge(sem["sT"], tprev)
                            for mc in range(HC):
                                if mc >= 2:
                                    sync.wait_ge(sem[fc_name], fcb + mc - 1)
                                inc("qld", sync.dma_start(
                                    out=wdb[mc % 2][:],
                                    in_=wsrc.rearrange("(kc p) (mc m) -> p kc mc m", p=128, m=128)[:, :, mc, :]), 16)
                                w8(sync, "qld")
                        block.sync(_s_w)

                        tb = cnt["sT"]

                        def _t_w(tensor):
                            for mc in range(HC):
                                tensor.wait_ge(sem["qld"], qb + 16 * (mc + 1))
                                if mc < 2:
                                    tensor.wait_ge(sem[fc_name], fcb)
                                else:
                                    tensor.wait_ge(sem[fc_name], fcb + mc - 1)
                                for kc in range(HC):
                                    tensor.matmul(pb[mc % 2][:, 0:512], wdb[mc % 2][:, kc, :],
                                                  rhs_buf[:, kc, :], start=(kc == 0), stop=(kc == HC - 1))
                                inc("sT", tensor.drain())
                        block.tensor(_t_w)

                        def _c_e(scalar):
                            scalar.drain()
                            for mc in range(HC):
                                scalar.wait_ge(sem["sT"], tb + mc + 1)
                                inc(fc_name, scalar.activation(out_buf[:, mc, :], pb[mc % 2][:, 0:512],
                                                               AF.Gelu, scale=s_scale[:, mc:mc + 1]))
                        block.scalar(_c_e)

                    if p3_sub >= 7:
                        ffn_big("h1", h1WT, ht, gt, s_h1s)
                    if p3_sub >= 8:
                        ffn_big("h2", h2WT, gt, ht, s_h2s)

                    fcb3 = cnt["fC"]
                    qb3 = cnt["qld"]

                    tprev3 = cnt["sT"]

                    def _s_w3(sync):
                        sync.wait_ge(sem["sT"], tprev3)
                        for mc in range(DC):
                            if mc >= 2:
                                sync.wait_ge(sem["fC"], fcb3 + mc - 1)
                            inc("qld", sync.dma_start(
                                out=wdb[mc % 2][:],
                                in_=w3WT.rearrange("(kc p) (mc m) -> p kc mc m", p=128, m=128)[:, :, mc, :]), 16)
                            w8(sync, "qld")
                    if p3_sub >= 8:
                        block.sync(_s_w3)

                    tb3 = cnt["sT"]

                    def _t_w3(tensor):
                        for mc in range(DC):
                            tensor.wait_ge(sem["qld"], qb3 + 16 * (mc + 1))
                            if mc < 2:
                                tensor.wait_ge(sem["fC"], fcb3)
                            else:
                                tensor.wait_ge(sem["fC"], fcb3 + mc - 1)
                            for kc in range(HC):
                                tensor.matmul(pb[mc % 2][:, 0:512], wdb[mc % 2][:, kc, :],
                                              ht[:, kc, :], start=(kc == 0), stop=(kc == HC - 1))
                            inc("sT", tensor.drain())
                    if p3_sub >= 8:
                        block.tensor(_t_w3)

                    def _v_w3(vector, tb3=tb3):
                        vector.drain()
                        for mc in range(DC):
                            vector.wait_ge(sem["sT"], tb3 + mc + 1)
                            inc("fC", vector.scalar_tensor_tensor(
                                GA[:, mc, 0:PT], pb[mc % 2][:, 0:512], s_w3s[:, mc:mc + 1],
                                GF[:, mc, 0:PT], ALU.mult, ALU.add))
                    if p3_sub >= 8:
                        block.vector(_v_w3)

                    def _s_out(sync, cs=cs):
                        w8(sync, "fC")
                        inc("qst", sync.dma_start(
                            out=out_fm.rearrange("(c p) t -> p c t", p=128)[:, :, cs],
                            in_=GA[:, :, 0:PT]), 16)
                        w8(sync, "qst")
                    if p3_sub >= 9:
                        block.sync(_s_out)

                    def _v_g3(vector):
                        w8(vector, "qst")
                        inc("gate", vector.drain())
                    if p3_sub >= 1:
                        block.vector(_v_g3)

        def _s_fin(sync):
            w8(sync, "gate")
            w8(sync, "sV")
            inc("qst", sync.dma_start(out=oid, in_=mk[:]), 16)
            for d_out, d_src in dbg:
                inc("qst", sync.dma_start(out=d_out, in_=d_src), 16)
            w8(sync, "qst")
        block.sync(_s_fin)

    return nc


def prep_inputs(inputs):
    f32 = np.float32
    bf = ml_dtypes.bfloat16
    x = np.asarray(inputs["L0_attn_x"], f32)[0]
    xT = np.ascontiguousarray(x.T)
    shared = {
        "mtab": np.arange(8, dtype=f32).reshape(8, 1),
        "inWT": np.ascontiguousarray(np.asarray(inputs["L0_attn_in_proj_w"], f32).T).astype(bf),
        "in_b_pack": np.ascontiguousarray(np.asarray(inputs["L0_attn_in_proj_b"], f32).reshape(16, 128).T),
        "cw_pack": np.ascontiguousarray(
            np.asarray(inputs["L0_attn_conv_w"], f32)[:, 0, :].reshape(8, 128, 4).transpose(1, 0, 2).reshape(128, 32)),
        "cb_pack": np.ascontiguousarray(np.asarray(inputs["L0_attn_conv_b"], f32).reshape(8, 128).T),
        "A_log_full": np.ascontiguousarray(np.asarray(inputs["L0_attn_A_log"], f32)),
        "bcWT": np.ascontiguousarray(
            np.concatenate([np.asarray(inputs["L0_attn_B_w"], f32),
                            np.asarray(inputs["L0_attn_C_w"], f32)], 0).T).astype(bf),
        "bcb": np.concatenate([np.asarray(inputs["L0_attn_B_b"], f32),
                               np.asarray(inputs["L0_attn_C_b"], f32)]).reshape(64, 1),
        "dtWT": np.ascontiguousarray(np.asarray(inputs["L0_attn_dt_w"], f32).T).astype(bf),
        "dtb_pack": np.ascontiguousarray(np.asarray(inputs["L0_attn_dt_b"], f32).reshape(8, 128).T),
        "outWT": np.ascontiguousarray(np.asarray(inputs["L0_attn_out_w"], f32).T).astype(bf),
        "outb_pack": np.ascontiguousarray(np.asarray(inputs["L0_attn_out_b"], f32).reshape(8, 128).T),
        "g_pack": np.ascontiguousarray(np.asarray(inputs["L0_attn_ln_g"], f32).reshape(8, 128).T),
        "beta_pack": np.ascontiguousarray(np.asarray(inputs["L0_attn_ln_beta"], f32).reshape(8, 128).T),
        "w1WT": np.ascontiguousarray(np.asarray(inputs["L0_ffn_w1_w"], f32).T).astype(bf),
        "w2WT": np.ascontiguousarray(np.asarray(inputs["L0_ffn_w2_w"], f32).T).astype(bf),
        "h1WT": np.ascontiguousarray(np.asarray(inputs["L0_ffn_h1_w"], f32).T).astype(bf),
        "h2WT": np.ascontiguousarray(np.asarray(inputs["L0_ffn_h2_w"], f32).T).astype(bf),
        "w3WT": np.ascontiguousarray(np.asarray(inputs["L0_ffn_w3_w"], f32).T).astype(bf),
        "w1s_pack": np.ascontiguousarray(np.asarray(inputs["L0_ffn_w1_s"], f32).reshape(32, 128).T),
        "w2s_pack": np.ascontiguousarray(np.asarray(inputs["L0_ffn_w2_s"], f32).reshape(32, 128).T),
        "h1s_pack": np.ascontiguousarray(np.asarray(inputs["L0_ffn_h1_s"], f32).reshape(32, 128).T),
        "h2s_pack": np.ascontiguousarray(np.asarray(inputs["L0_ffn_h2_s"], f32).reshape(32, 128).T),
        "w3s_pack": np.ascontiguousarray(np.asarray(inputs["L0_ffn_w3_s"], f32).reshape(8, 128).T),
        "oh64": np.eye(64, dtype=f32),
    }
    in_maps = []
    for l in range(R):
        xe = np.zeros((D, TL + 4), f32)
        g0 = TL * l - 2
        lo = max(0, g0)
        hi = min(T, g0 + TL + 4)
        xe[:, lo - g0:hi - g0] = xT[:, lo:hi]
        m3 = np.ones((128, 3), f32)
        if l == 0:
            m3[:, 0:2] = 0.0
        if l == R - 1:
            m3[:, 2] = 0.0
        im = {"x_ext": xe, "mask3": m3}
        im.update(shared)
        in_maps.append(im)
    return in_maps


def kernel(**inputs):
    in_maps = prep_inputs(inputs)
    nc = build()
    res = run_bass_kernel_spmd(nc, in_maps, core_ids=list(range(R)))
    outT = np.zeros((D, T), np.float32)
    seen = []
    for l in range(R):
        ml = int(round(float(res.results[l]["oid"][0, 0])))
        seen.append(ml)
        outT[:, TL * ml:TL * (ml + 1)] = res.results[l]["out_fm"]
    assert sorted(seen) == list(range(R)), f"slice ids not a permutation: {seen}"
    return np.ascontiguousarray(outT.T)[None]


# version 04 (final)
# speedup vs baseline: 1.0201x; 1.0201x over previous
import contextlib
import numpy as np
import ml_dtypes
import concourse.bass as bass
import concourse.mybir as mybir
from concourse.bass_utils import run_bass_kernel_spmd

F32 = mybir.dt.float32
BF16 = mybir.dt.bfloat16
AF = mybir.ActivationFunctionType
ALU = mybir.AluOpType
AX = mybir.AxisListType

R = 8
TL = 2048
T = R * TL
D = 1024
DC = 8
NST = 32
H = 4096
HC = 32
W = 512
NWIN = T // W
WPC = TL // W
NG = 4
GS = NST // NG
PT = 512
EXT = PT + 4
NT1 = TL // PT
BM0 = 3 * D
CM0 = 3 * D + 32


def build(debug=(), stages=3, n_tiles=NT1, do_ag=True, do_compact=True, p1_sub=99, p3_sub=99):
    nc = bass.Bass("TRN2", target_bir_lowering=False)

    def din(name, shape, dt):
        return nc.dram_tensor(name, shape, dt, kind="ExternalInput").ap()

    def dint(name, shape, dt, shared=False):
        kw = dict(kind="Internal")
        if shared:
            kw["addr_space"] = "Shared"
        return nc.dram_tensor(name, shape, dt, **kw).ap()

    def dout(name, shape, dt):
        return nc.dram_tensor(name, shape, dt, kind="ExternalOutput").ap()

    x_ext = din("x_ext", [D, TL + 4], F32)
    mask3 = din("mask3", [128, 3], F32)
    mtab = din("mtab", [8, 1], F32)
    inWT = din("inWT", [D, 2 * D], BF16)
    in_b_pack = din("in_b_pack", [128, 16], F32)
    cw_pack = din("cw_pack", [128, 32], F32)
    cb_pack = din("cb_pack", [128, 8], F32)
    A_log_full = din("A_log_full", [D, NST], F32)
    bcWT = din("bcWT", [D, 64], BF16)
    bcb = din("bcb", [64, 1], F32)
    dtWT = din("dtWT", [D, D], BF16)
    dtb_pack = din("dtb_pack", [128, 8], F32)
    outWT = din("outWT", [D, D], BF16)
    outb_pack = din("outb_pack", [128, 8], F32)
    g_pack = din("g_pack", [128, 8], F32)
    beta_pack = din("beta_pack", [128, 8], F32)
    w1WT = din("w1WT", [HC, 128, DC * 128], BF16)
    w2WT = din("w2WT", [HC, 128, DC * 128], BF16)
    h1WT = din("h1WT", [HC, 128, HC * 128], BF16)
    h2WT = din("h2WT", [HC, 128, HC * 128], BF16)
    w3WT = din("w3WT", [DC, 128, HC * 128], BF16)
    w1s_pack = din("w1s_pack", [128, 32], F32)
    w2s_pack = din("w2s_pack", [128, 32], F32)
    h1s_pack = din("h1s_pack", [128, 32], F32)
    h2s_pack = din("h2s_pack", [128, 32], F32)
    w3s_pack = din("w3s_pack", [128, 8], F32)
    oh64 = din("oh64", [64, 64], F32)

    agA_in = dint("agA_in", [2 * D, TL], F32)
    agA_out = dint("agA_out", [R, 2 * D, TL], F32, shared=True)
    agAb_in = dint("agAb_in", [D, TL], BF16)
    agAb_out = dint("agAb_out", [R, D, TL], BF16, shared=True)
    agB_in = dint("agB_in", [3 * D + 64, TL], F32)
    agB_out = dint("agB_out", [R, 3 * D + 64, TL], F32, shared=True)
    agY_in = dint("agY_in", [128, T + D], BF16)
    agY_out = dint("agY_out", [R, 128, T + D], BF16, shared=True)
    c_dt = dint("c_dt", [128, T], F32)
    c_u = dint("c_u", [128, T], F32)
    c_sz = dint("c_sz", [128, T], F32)
    c_x = dint("c_x", [D, TL], F32)
    c_xn = dint("c_xn", [D, TL], F32)
    c_xnbf = dint("c_xnbf", [D, TL], BF16)
    c_yg = dint("c_yg", [D, TL], BF16)
    internal = {"c_dt": c_dt, "c_u": c_u, "c_sz": c_sz, "c_x": c_x, "c_xn": c_xn,
                "c_xnbf": c_xnbf, "c_yg": c_yg, "agB_in": agB_in, "agY_in": agY_in,
                "agA_in": agA_in}

    out_fm = dout("out_fm", [D, TL], F32)
    oid = dout("oid", [1, 1], F32)
    dbg = [(dout("dbg_" + nm, list(internal[nm].shape), internal[nm].dtype), internal[nm])
           for nm in debug]

    with contextlib.ExitStack() as STK:
        def sb(n, s, d):
            return STK.enter_context(nc.sbuf_tensor(n, s, d))

        GA = sb("GA", [128, DC, EXT], F32)
        GB = sb("GB", [128, DC, EXT], F32)
        GC = sb("GC", [128, DC, EXT], F32)
        GD = sb("GDp", [128, DC * EXT + 16], F32).ap().rearrange(
            "p (pad) -> p pad")[:, 0:DC * EXT].rearrange("p (c e) -> p c e", c=DC) if False else sb("GD", [128, DC, EXT + 4], F32)[:, :, 0:EXT]
        GE = sb("GE", [128, DC, EXT], F32)
        GF = sb("GF", [128, DC, EXT], F32)
        ones_col = sb("ones_col", [128, 1], F32)
        ones_row = sb("ones_row", [1, 128], F32)
        mk = sb("mk", [1, 1], F32)
        s_inb = sb("s_inb", [128, 16], F32)
        s_cw = sb("s_cw", [128, 32], F32)
        s_cb = sb("s_cb", [128, 8], F32)
        s_dtb = sb("s_dtb", [128, 8], F32)
        s_bcb = sb("s_bcb", [64, 1], F32)
        s_outb = sb("s_outb", [128, 8], F32)
        s_g = sb("s_g", [128, 8], F32)
        s_beta = sb("s_beta", [128, 8], F32)
        s_w1s = sb("s_w1s", [128, 32], F32)
        s_w2s = sb("s_w2s", [128, 32], F32)
        s_h1s = sb("s_h1s", [128, 32], F32)
        s_h2s = sb("s_h2s", [128, 32], F32)
        s_w3s = sb("s_w3s", [128, 8], F32)
        s_mask = sb("s_mask", [128, 3], F32)
        s_Alog = sb("s_Alog", [128, NST], F32)
        s_Apos = sb("s_Apos", [128, NST], F32)
        s_Aneg = sb("s_Aneg", [128, NST], F32)
        vcar = sb("vcar", [128, NST], F32)
        carry_t = sb("carry_t", [128, GS], F32)
        stat = sb("stat", [1, 3 * EXT], F32)
        xnbft = sb("xnbft", [128, DC, PT], BF16)
        s_oh = sb("s_oh", [64, 64], F32)

        pb = [STK.enter_context(nc.psum_tensor(f"pb{i}", [128, W], F32)) for i in range(8)]

        SEMS = ["qld", "qst", "sT", "sV", "sS", "fB", "fC", "sG", "gate"]
        sem = {nm: STK.enter_context(nc.semaphore(nm)) for nm in SEMS}
        cnt = {nm: 0 for nm in SEMS}

        block = STK.enter_context(nc.Block())

        def inc(nm, instr, k=1):
            instr.then_inc(sem[nm], k)
            cnt[nm] += k
            return instr

        def w8(eng, nm, val=None):
            eng.wait_ge(sem[nm], cnt[nm] if val is None else int(val))

        def bcd(ap2d):
            return ap2d.unsqueeze(1).broadcast_to([ap2d.shape[0], DC, ap2d.shape[1]])

        def _pre_s(sync):
            for t_, src in [(s_inb, in_b_pack), (s_cw, cw_pack), (s_cb, cb_pack),
                            (s_dtb, dtb_pack), (s_bcb, bcb), (s_outb, outb_pack),
                            (s_g, g_pack), (s_beta, beta_pack), (s_w1s, w1s_pack),
                            (s_w2s, w2s_pack), (s_h1s, h1s_pack), (s_h2s, h2s_pack),
                            (s_w3s, w3s_pack), (s_mask, mask3), (s_oh, oh64)]:
                inc("qld", sync.dma_start(out=t_[:], in_=src), 16)
            k_add = [0]
            for arm in sync.switch_core_id(128):
                ml = arm.logical % 8
                a1 = sync.dma_start(out=s_Alog[:], in_=A_log_full[128 * ml:128 * (ml + 1), :])
                a1.then_inc(sem["qld"], 16)
                a2 = sync.dma_start(out=mk[:], in_=mtab[ml:ml + 1, :])
                a2.then_inc(sem["qld"], 16)
                a3 = sync.dma_start(out=agY_in[:, T:], in_=outWT[128 * ml:128 * (ml + 1), :])
                a3.then_inc(sem["qld"], 16)
                k_add[0] = 48
            cnt["qld"] += k_add[0]
        block.sync(_pre_s)

        def _pre_v(vector):
            vector.memset(ones_col[:], 1.0)
            vector.memset(ones_row[:], 1.0)
            vector.memset(vcar[:], 0.0)
            w8(vector, "qld")
            inc("sV", vector.tensor_scalar(s_Apos[:], s_Alog[:], -5.0, 2.0, ALU.max, ALU.min))
        block.vector(_pre_v)

        def _pre_c(scalar):
            w8(scalar, "sV")
            inc("sS", scalar.activation(s_Apos[:], s_Apos[:], AF.Exp))
        block.scalar(_pre_c)

        def _pre_v2(vector):
            w8(vector, "sS")
            inc("sV", vector.tensor_scalar(s_Aneg[:], s_Apos[:], -1.0, None, ALU.mult))
        block.vector(_pre_v2)

        with contextlib.ExitStack() as P1S:
            def sb1(n, s, d):
                return P1S.enter_context(nc.sbuf_tensor(n, s, d))
            SC1 = sb1("SC1", [128, W], F32)
            SC2 = sb1("SC2", [64, W], F32)
            s_inW = sb1("s_inW", [128, DC, 16, 128], BF16)
            s_dtW = sb1("s_dtW", [128, DC, DC, 128], BF16)
            s_bcW = sb1("s_bcW", [128, DC, 64], BF16)
            xnbfe = sb1("xnbfe", [128, DC, EXT], BF16)

            def _w_s(sync):
                w8(sync, "qld")
                inc("qld", sync.dma_start(
                    out=s_inW[:],
                    in_=inWT.rearrange("(kc p) (mc m) -> p kc mc m", p=128, m=128)), 16)
                inc("qld", sync.dma_start(
                    out=s_dtW[:],
                    in_=dtWT.rearrange("(kc p) (mc m) -> p kc mc m", p=128, m=128)), 16)
                inc("qld", sync.dma_start(
                    out=s_bcW[:], in_=bcWT.rearrange("(kc p) m -> p kc m", p=128)), 16)
            block.sync(_w_s)

            for k in range(n_tiles):
                x_t, tmp_t, xn_t, xc_t, xs_t, dt_t = GA, GB, GC, GD, GE, GF
                c0 = PT * k

                def _s_x(sync, k=k):
                    w8(sync, "gate")
                    inc("qld", sync.dma_start(
                        out=GA[:],
                        in_=x_ext.rearrange("(c p) t -> p c t", p=128)[:, :, PT * k: PT * k + EXT]), 16)
                    w8(sync, "qld")
                if p1_sub >= 1:
                    block.sync(_s_x)

                def _c_sq(scalar):
                    w8(scalar, "qld")
                    scalar.drain()
                    inc("sS", scalar.activation(GB[:], GA[:], AF.Square))
                if p1_sub >= 2:
                    block.scalar(_c_sq)

                def _t_stats(tensor):
                    w8(tensor, "sS")
                    for c in range(DC):
                        fl, ll = (c == 0), (c == DC - 1)
                        tensor.matmul(pb[4][0:1, 0:512], ones_col[:], GA[:, c, 0:512], start=fl, stop=ll)
                        tensor.matmul(pb[5][0:1, 0:4], ones_col[:], GA[:, c, 512:516], start=fl, stop=ll)
                        tensor.matmul(pb[6][0:1, 0:512], ones_col[:], GB[:, c, 0:512], start=fl, stop=ll)
                        mm = tensor.matmul(pb[7][0:1, 0:4], ones_col[:], GB[:, c, 512:516], start=fl, stop=ll)
                    inc("sT", mm)
                if p1_sub >= 3:
                    block.tensor(_t_stats)

                mean_v = stat[0:1, 0:EXT]
                ex2_v = stat[0:1, EXT:2 * EXT]
                isg_v = stat[0:1, EXT:2 * EXT]
                nmis_v = stat[0:1, 2 * EXT:3 * EXT]

                def _v_stats(vector):
                    w8(vector, "sT")
                    vector.tensor_scalar(mean_v[:, 0:512], pb[4][0:1, 0:512], 1.0 / D, None, ALU.mult)
                    vector.tensor_scalar(mean_v[:, 512:516], pb[5][0:1, 0:4], 1.0 / D, None, ALU.mult)
                    vector.tensor_scalar(ex2_v[:, 0:512], pb[6][0:1, 0:512], 1.0 / D, None, ALU.mult)
                    vector.tensor_scalar(ex2_v[:, 512:516], pb[7][0:1, 0:4], 1.0 / D, None, ALU.mult)
                    vector.drain()
                    vector.tensor_tensor(nmis_v, mean_v, mean_v, ALU.mult)
                    vector.drain()
                    vector.tensor_tensor(ex2_v, ex2_v, nmis_v, ALU.subtract)
                    vector.drain()
                    inc("sV", vector.tensor_scalar(ex2_v, ex2_v, 1e-5, None, ALU.add))
                if p1_sub >= 4:
                    block.vector(_v_stats)

                def _c_isg(scalar):
                    w8(scalar, "sV")
                    scalar.drain()
                    inc("sS", scalar.activation(isg_v, ex2_v, AF.Ln))
                if p1_sub >= 5:
                    block.scalar(_c_isg)

                def _v_half(vector):
                    w8(vector, "sS")
                    vector.drain()
                    inc("sV", vector.tensor_scalar(isg_v, isg_v, -0.5, None, ALU.mult))
                if p1_sub >= 5:
                    block.vector(_v_half)

                def _c_isg2(scalar):
                    w8(scalar, "sV")
                    scalar.drain()
                    inc("sS", scalar.activation(isg_v, isg_v, AF.Exp))
                if p1_sub >= 5:
                    block.scalar(_c_isg2)

                def _v_nmis(vector):
                    w8(vector, "sS")
                    vector.drain()
                    inc("sV", vector.scalar_tensor_tensor(nmis_v, mean_v, -1.0, isg_v,
                                                          ALU.mult, ALU.mult))
                if p1_sub >= 6:
                    block.vector(_v_nmis)

                def _t_bc(tensor):
                    w8(tensor, "sV")
                    tensor.matmul(pb[4][:, 0:512], ones_row[:], isg_v[:, 0:512], start=True, stop=True)
                    tensor.matmul(pb[5][:, 0:4], ones_row[:], isg_v[:, 512:516], start=True, stop=True)
                    tensor.matmul(pb[6][:, 0:512], ones_row[:], nmis_v[:, 0:512], start=True, stop=True)
                    inc("sT", tensor.matmul(pb[7][:, 0:4], ones_row[:], nmis_v[:, 512:516],
                                            start=True, stop=True))
                if p1_sub >= 7:
                    block.tensor(_t_bc)

                def _v_xn(vector):
                    w8(vector, "sT")
                    vector.tensor_tensor(GC[:, :, 0:512], GA[:, :, 0:512], bcd(pb[4][:, 0:512]), ALU.mult)
                    vector.tensor_tensor(GC[:, :, 512:516], GA[:, :, 512:516], bcd(pb[5][:, 0:4]), ALU.mult)
                    vector.drain()
                    vector.tensor_tensor(GC[:, :, 0:512], GC[:, :, 0:512], bcd(pb[6][:, 0:512]), ALU.add)
                    vector.tensor_tensor(GC[:, :, 512:516], GC[:, :, 512:516], bcd(pb[7][:, 0:4]), ALU.add)
                    vector.drain()
                    vector.tensor_copy(xnbfe[:], GC[:])
                    inc("sV", vector.tensor_copy(xnbft[:], GC[:, :, 2:514]))
                if p1_sub >= 8:
                    block.vector(_v_xn)

                def _s_stA(sync, k=k):
                    w8(sync, "sV")
                    cs = slice(c0, c0 + PT)
                    inc("qst", sync.dma_start(
                        out=agA_in[0:D, cs].rearrange("(c p) t -> p c t", p=128),
                        in_=GA[:, :, 2:514]), 16)
                    inc("qst", sync.dma_start(
                        out=agA_in[D:2 * D, cs].rearrange("(c p) t -> p c t", p=128),
                        in_=GC[:, :, 2:514]), 16)
                    inc("qst", sync.dma_start(
                        out=agAb_in[:, cs].rearrange("(c p) t -> p c t", p=128),
                        in_=xnbft[:]), 16)
                if p1_sub >= 9:
                    block.sync(_s_stA)

                t_base = cnt["sT"]
                fB_base = cnt["fB"]

                def _t_ip(tensor):
                    w8(tensor, "sV")
                    for mc in range(16):
                        bank = mc % 4
                        if mc >= 4:
                            tensor.wait_ge(sem["fB"], fB_base + mc - 3)
                        if mc < 8:
                            for kc in range(DC):
                                tensor.matmul(pb[bank][:, 0:512], s_inW[:, kc, mc, :],
                                              xnbfe[:, kc, 0:512], start=(kc == 0), stop=(kc == DC - 1))
                            for kc in range(DC):
                                mm = tensor.matmul(pb[4 + bank][:, 0:16], s_inW[:, kc, mc, :],
                                                   xnbfe[:, kc, 500:516], start=(kc == 0), stop=(kc == DC - 1))
                        else:
                            for kc in range(DC):
                                mm = tensor.matmul(pb[bank][:, 0:512], s_inW[:, kc, mc, :],
                                                   xnbfe[:, kc, 2:514], start=(kc == 0), stop=(kc == DC - 1))
                        inc("sT", mm)
                if p1_sub >= 10:
                    block.tensor(_t_ip)

                def _v_ipe(vector):
                    for mc in range(8):
                        vector.wait_ge(sem["sT"], t_base + mc + 1)
                        vector.tensor_scalar(GD[:, mc, 0:512], pb[mc % 4][:, 0:512],
                                             s_inb[:, mc:mc + 1], None, ALU.add)
                        inc("fB", vector.tensor_scalar(GD[:, mc, 512:516], pb[4 + mc % 4][:, 12:16],
                                                       s_inb[:, mc:mc + 1], None, ALU.add))
                if p1_sub >= 10:
                    block.vector(_v_ipe)

                def _c_z(scalar):
                    scalar.drain()
                    scalar.wait_ge(sem["fB"], fB_base + 8)
                    for mc in range(8, 16):
                        scalar.wait_ge(sem["sT"], t_base + mc + 1)
                        inc("fB", scalar.activation(GB[:, mc - 8, 0:512], pb[mc % 4][:, 0:512],
                                                    AF.Silu, bias=s_inb[:, mc:mc + 1]))
                if p1_sub >= 10:
                    block.scalar(_c_z)

                def _s_stz(sync, k=k):
                    w8(sync, "fB")
                    inc("qst", sync.dma_start(
                        out=agB_in[2 * D:3 * D, c0:c0 + PT].rearrange("(c p) t -> p c t", p=128),
                        in_=GB[:, :, 0:512]), 16)
                if p1_sub >= 11:
                    block.sync(_s_stz)

                def _v_conv(vector, k=k):
                    vector.drain()
                    if k == 0:
                        for c in range(DC):
                            vector.tensor_tensor(GD[:, c, 0:2], GD[:, c, 0:2],
                                                 s_mask[:, 0:2], ALU.mult)
                    if k == NT1 - 1:
                        for c in range(DC):
                            vector.tensor_tensor(GD[:, c, 514:515], GD[:, c, 514:515],
                                                 s_mask[:, 2:3], ALU.mult)
                    if k == 0 or k == NT1 - 1:
                        vector.drain()
                    for c in range(DC):
                        vector.tensor_scalar(GE[:, c, 0:512], GD[:, c, 0:512],
                                             s_cw[:, 4 * c:4 * c + 1], None, ALU.mult)
                        for m in range(1, 4):
                            vector.drain()
                            vector.scalar_tensor_tensor(GE[:, c, 0:512], GD[:, c, m:512 + m],
                                                        s_cw[:, 4 * c + m:4 * c + m + 1],
                                                        GE[:, c, 0:512], ALU.mult, ALU.add)
                    inc("sV", vector.drain())
                if p1_sub >= 12:
                    block.vector(_v_conv)

                def _c_xs(scalar):
                    w8(scalar, "sV")
                    scalar.drain()
                    for c in range(DC):
                        a = scalar.activation(GE[:, c, 0:512], GE[:, c, 0:512], AF.Silu,
                                              bias=s_cb[:, c:c + 1])
                    inc("sS", a)
                if p1_sub >= 13:
                    block.scalar(_c_xs)

                def _v_xsbf(vector):
                    w8(vector, "sS")
                    vector.drain()
                    inc("sV", vector.tensor_copy(xnbfe[:, :, 0:512], GE[:, :, 0:512]))
                if p1_sub >= 14:
                    block.vector(_v_xsbf)

                t2 = cnt["sT"]
                fb2 = cnt["fB"]

                def _t_dt(tensor, fb2=fb2):
                    w8(tensor, "sV")
                    for mc in range(8):
                        if mc < 4:
                            tensor.wait_ge(sem["fB"], fb2)
                        else:
                            tensor.wait_ge(sem["fB"], fb2 + mc - 3)
                        for kc in range(DC):
                            mm = tensor.matmul(pb[mc % 4][:, 0:512], s_dtW[:, kc, mc, :],
                                               xnbfe[:, kc, 0:512], start=(kc == 0), stop=(kc == DC - 1))
                        inc("sT", mm)
                if p1_sub >= 15:
                    block.tensor(_t_dt)

                def _cv_dt(scalar, t2=t2, fb2=fb2):
                    for mc in range(8):
                        scalar.wait_ge(sem["sT"], t2 + mc + 1)
                        if mc >= 1:
                            scalar.wait_ge(sem["fB"], fb2 + mc)
                        scalar.drain()
                        scalar.activation(SC1[:], pb[mc % 4][:, 0:512], AF.Exp,
                                          bias=s_dtb[:, mc:mc + 1])
                        scalar.drain()
                        inc("sS", scalar.activation(SC1[:], SC1[:], AF.Ln, bias=1.0))
                if p1_sub >= 15:
                    block.scalar(_cv_dt)
                s2 = cnt["sS"] - 8

                def _v_dt(vector):
                    vector.wait_ge(sem["fB"], fb2)
                    for mc in range(8):
                        vector.wait_ge(sem["sS"], s2 + mc + 1)
                        inc("fB", vector.tensor_scalar(GF[:, mc, 0:512], SC1[:], 0.001, 0.2,
                                                       ALU.max, ALU.min))
                if p1_sub >= 15:
                    block.vector(_v_dt)

                def _t_bcm(tensor):
                    w8(tensor, "fB")
                    for kc in range(DC):
                        mm = tensor.matmul(pb[4][0:64, 0:512], s_bcW[:, kc, :],
                                           xnbfe[:, kc, 0:512], start=(kc == 0), stop=(kc == DC - 1))
                    inc("sT", mm)
                if p1_sub >= 16:
                    block.tensor(_t_bcm)

                def _v_bcm(vector):
                    w8(vector, "sT")
                    vector.drain()
                    vector.tensor_scalar(SC2[0:32, :], pb[4][0:32, 0:512], s_bcb[0:32, :], -1.0,
                                         ALU.add, ALU.mult)
                    vector.tensor_scalar(SC2[32:64, :], pb[4][32:64, 0:512], s_bcb[32:64, :], None,
                                         ALU.add)
                    w8(vector, "qst")
                    inc("sV", vector.tensor_tensor(GC[:, :, 0:512], GF[:, :, 0:512],
                                                   GE[:, :, 0:512], ALU.mult))
                if p1_sub >= 16:
                    block.vector(_v_bcm)

                def _s_stB(sync, k=k):
                    w8(sync, "sV")
                    w8(sync, "qst")
                    cs = slice(c0, c0 + PT)
                    inc("qst", sync.dma_start(
                        out=agB_in[0:D, cs].rearrange("(c p) t -> p c t", p=128),
                        in_=GF[:, :, 0:512]), 16)
                    inc("qst", sync.dma_start(
                        out=agB_in[D:2 * D, cs].rearrange("(c p) t -> p c t", p=128),
                        in_=GC[:, :, 0:512]), 16)
                    inc("qst", sync.dma_start(out=agB_in[BM0:BM0 + 64, cs], in_=SC2[:]), 16)
                    w8(sync, "qst")
                if p1_sub >= 17:
                    block.sync(_s_stB)

                def _v_gate(vector):
                    w8(vector, "qst")
                    inc("gate", vector.drain())
                if p1_sub >= 18:
                    block.vector(_v_gate)

        def _s_p1end(sync):
            w8(sync, "qld")
            w8(sync, "qst")
        block.sync(_s_p1end)

        def _g_ag(gpsimd):
            w8(gpsimd, "qst")
            w8(gpsimd, "qld")
            if not do_ag:
                return
            cc = gpsimd.collective_compute("AllGather", ALU.bypass, ins=[agA_in],
                                           outs=[agA_out], replica_groups=[list(range(R))])
            inc("sG", cc)
            cc = gpsimd.collective_compute("AllGather", ALU.bypass, ins=[agAb_in],
                                           outs=[agAb_out], replica_groups=[list(range(R))])
            inc("sG", cc)
            cc = gpsimd.collective_compute("AllGather", ALU.bypass, ins=[agB_in],
                                           outs=[agB_out], replica_groups=[list(range(R))])
            inc("sG", cc)
            w8(gpsimd, "sG")
        block.gpsimd(_g_ag)

        def _s_compact(sync):
            w8(sync, "sG")
            if not do_compact:
                return
            k_add = [0]
            for arm in sync.switch_core_id(128):
                ml = arm.logical % 8
                n = 0
                d0 = 128 * ml
                a = sync.dma_start(out=c_x, in_=agA_out[ml, 0:D, :])
                a.then_inc(sem["qld"], 16); n += 16
                a = sync.dma_start(out=c_xn, in_=agA_out[ml, D:2 * D, :])
                a.then_inc(sem["qld"], 16); n += 16
                a = sync.dma_start(out=c_xnbf, in_=agAb_out[ml])
                a.then_inc(sem["qld"], 16); n += 16
                for r in range(R):
                    cs = slice(TL * r, TL * (r + 1))
                    a = sync.dma_start(out=c_dt[:, cs], in_=agB_out[r, d0:d0 + 128, :])
                    a.then_inc(sem["qld"], 16); n += 16
                    a = sync.dma_start(out=c_u[:, cs], in_=agB_out[r, D + d0:D + d0 + 128, :])
                    a.then_inc(sem["qld"], 16); n += 16
                    a = sync.dma_start(out=c_sz[:, cs], in_=agB_out[r, 2 * D + d0:2 * D + d0 + 128, :])
                    a.then_inc(sem["qld"], 16); n += 16
                k_add[0] = n
            cnt["qld"] += k_add[0]
        block.sync(_s_compact)

        if stages >= 2:
            with contextlib.ExitStack() as P2S:
                def sb2(n, s, d):
                    return P2S.enter_context(nc.sbuf_tensor(n, s, d))
                zeros_w = sb2("zeros_w", [128, W], F32)
                lastc = sb2("lastc", [128, 32], F32)
                dtw = [sb2(f"dtw{i}", [128, W], F32) for i in range(2)]
                uw = [sb2(f"uw{i}", [128, W], F32) for i in range(2)]
                szw = [sb2(f"szw{i}", [128, W], F32) for i in range(2)]
                bcw = [sb2(f"bcw{i}", [64, W], F32) for i in range(2)]
                ywin = sb2("ywin", [128, W], F32)
                ygw = [sb2(f"ygw{i}", [128, W], BF16) for i in range(2)]
                dwin = sb2("dwin", [128, W], F32)
                rs_t = sb2("rs_t", [128, W], F32)
                EmP = [GA[:, :, 0:W], GB[:, :, 0:W]]
                EpP = [GC[:, :, 0:W], GD[:, :, 0:W]]
                qg_b = GE[:, :, 0:W]
                NZ_b = GF[:, :, 0:W]
                LO_b = P2A = sb2("P2A", [128, GS, W], F32)
                HI_b = P2B = sb2("P2B", [128, GS, W], F32)
                vv_b = P2C = sb2("P2C", [128, GS, W], F32)

                def _v_z(vector):
                    vector.memset(zeros_w[:], 0.0)
                    vector.drain()
                block.vector(_v_z)

                ld0 = cnt["qld"]
                vwin_done = []
                store_done = []
                group_idx = 0
                sE_marks = []
                vE_marks = []

                for w in range(NWIN):
                    pw = w % 2
                    r = w // WPC
                    wc0 = (w % WPC) * W

                    def _s_ld(sync, w=w, pw=pw, r=r, wc0=wc0):
                        if w == 0:
                            sync.wait_ge(sem["qld"], ld0)
                        if w >= 2:
                            sync.wait_ge(sem["sV"], vwin_done[w - 2])
                        cs = slice(W * w, W * (w + 1))
                        inc("qld", sync.dma_start(out=dtw[pw][:], in_=c_dt[:, cs]), 16)
                        inc("qld", sync.dma_start(out=uw[pw][:], in_=c_u[:, cs]), 16)
                        inc("qld", sync.dma_start(out=szw[pw][:], in_=c_sz[:, cs]), 16)
                        inc("qld", sync.dma_start(
                            out=bcw[pw][:], in_=agB_out[r, BM0:BM0 + 64, wc0:wc0 + W]), 16)
                        w8(sync, "qld")
                    block.sync(_s_ld)

                    def _v_D(vector, w=w, pw=pw):
                        vector.wait_ge(sem["qld"], ld0 + 64 * (w + 1))
                        if w >= 1:
                            vector.wait_ge(sem["qst"], store_done[w - 1])
                        inc("sV", vector.tensor_tensor_scan(dwin[:], dtw[pw][:], zeros_w[:],
                                                            0.0, ALU.add, ALU.add))
                    block.vector(_v_D)
                    d_mark = cnt["sV"]

                    for g in range(NG):
                        gp = group_idx % 2
                        Em, Ep = EmP[gp], EpP[gp]

                        def _c_E(scalar, g=g, gp=gp, d_mark=d_mark, gi=group_idx):
                            scalar.wait_ge(sem["sV"], d_mark)
                            if gi >= 2:
                                scalar.wait_ge(sem["sV"], vE_marks[gi - 2])
                            scalar.drain()
                            for j in range(GS):
                                n = g * GS + j
                                scalar.activation(EmP[gp][:, j, :], dwin[:], AF.Exp,
                                                  scale=s_Apos[:, n:n + 1])
                                a = scalar.activation(EpP[gp][:, j, :], dwin[:], AF.Exp,
                                                      scale=s_Aneg[:, n:n + 1])
                            inc("sS", a)
                        block.scalar(_c_E)
                        sE_marks.append(cnt["sS"])

                        tb0 = cnt["sT"]
                        fb0 = cnt["fB"]
                        fc0 = cnt["fC"]

                        def _t_g(tensor, g=g, w=w, pw=pw, tb0=tb0, fb0=fb0, fc0=fc0):
                            tensor.wait_ge(sem["qld"], ld0 + 64 * (w + 1))
                            def ohc(col):
                                return s_oh[:, col:col + 1].broadcast_to([64, 128])
                            tensor.wait_ge(sem["fB"], fb0)
                            for j in range(4):
                                n = g * GS + j
                                inc("sT", tensor.matmul(pb[j][:, 0:W], ohc(n),
                                                        bcw[pw][:], start=True, stop=True))
                            tensor.wait_ge(sem["fC"], fc0)
                            for j in range(4):
                                n = g * GS + j
                                inc("sT", tensor.matmul(pb[4 + j][:, 0:W], ohc(32 + n),
                                                        bcw[pw][:], start=True, stop=True))
                            for j in range(4, 8):
                                n = g * GS + j
                                tensor.wait_ge(sem["fB"], fb0 + (j - 4) + 1)
                                inc("sT", tensor.matmul(pb[j % 4][:, 0:W], ohc(n),
                                                        bcw[pw][:], start=True, stop=True))
                            for j in range(4, 8):
                                n = g * GS + j
                                tensor.wait_ge(sem["fC"], fc0 + (j - 4) + 1)
                                inc("sT", tensor.matmul(pb[4 + j % 4][:, 0:W], ohc(32 + n),
                                                        bcw[pw][:], start=True, stop=True))
                        block.tensor(_t_g)

                        def _v_g(vector, g=g, gp=gp, pw=pw, w=w, tb0=tb0, gi=group_idx):
                            Em, Ep = EmP[gp], EpP[gp]
                            vector.wait_ge(sem["sS"], sE_marks[gi])
                            for j in range(GS):
                                if j < 4:
                                    vector.wait_ge(sem["sT"], tb0 + j + 1)
                                else:
                                    vector.wait_ge(sem["sT"], tb0 + 8 + (j - 4) + 1)
                                inc("fB", vector.tensor_tensor(qg_b[:, j, :], pb[j % 4][:, 0:W],
                                                               uw[pw][:], ALU.mult))
                            vector.drain()
                            vector.tensor_tensor(qg_b[:], qg_b[:], Em[:], ALU.mult)
                            vector.drain()
                            for j in range(GS):
                                vector.tensor_tensor_scan(NZ_b[:, j, :], qg_b[:, j, :], zeros_w[:],
                                                          0.0, ALU.add, ALU.add)
                            vector.drain()
                            vector.scalar_tensor_tensor(LO_b[:], Em[:], -10.0, NZ_b[:],
                                                        ALU.mult, ALU.add)
                            vector.scalar_tensor_tensor(HI_b[:], Em[:], 10.0, NZ_b[:],
                                                        ALU.mult, ALU.add)
                            vector.drain()
                            for j in range(GS):
                                n = g * GS + j
                                vector.tensor_tensor_scan(vv_b[:, j, :], LO_b[:, j, :], HI_b[:, j, :],
                                                          vcar[:, n:n + 1], ALU.max, ALU.min)
                            vector.drain()
                            vector.tensor_tensor(vv_b[:], vv_b[:], NZ_b[:], ALU.subtract)
                            vector.drain()
                            vector.tensor_tensor(vv_b[:], vv_b[:], Ep[:], ALU.mult)
                            vector.drain()
                            for j in range(GS):
                                n = g * GS + j
                                vector.tensor_copy(vcar[:, n:n + 1], vv_b[:, j, W - 1:W])
                            for j in range(GS):
                                if j < 4:
                                    vector.wait_ge(sem["sT"], tb0 + 4 + j + 1)
                                else:
                                    vector.wait_ge(sem["sT"], tb0 + 12 + (j - 4) + 1)
                                inc("fC", vector.tensor_tensor(LO_b[:, j, :], vv_b[:, j, :],
                                                               pb[4 + j % 4][:, 0:W], ALU.mult))
                            vector.drain()
                            red_in = LO_b[:].rearrange("p n t -> p t n")
                            if g == 0:
                                vector.tensor_reduce(ywin[:], red_in, AX.X, ALU.add)
                            else:
                                vector.tensor_reduce(rs_t[:], red_in, AX.X, ALU.add)
                                vector.drain()
                                vector.tensor_tensor(ywin[:], ywin[:], rs_t[:], ALU.add)
                            inc("sV", vector.drain())
                        block.vector(_v_g)
                        vE_marks.append(cnt["sV"])
                        group_idx += 1

                    def _v_yg(vector, w=w, pw=pw):
                        inc("sV", vector.tensor_tensor(ygw[pw][:], ywin[:], szw[pw][:], ALU.mult))
                    block.vector(_v_yg)

                    vwin_done.append(cnt["sV"])

                    def _s_sty(sync, w=w, pw=pw):
                        w8(sync, "sV")
                        inc("qst", sync.dma_start(out=agY_in[:, W * w:W * (w + 1)], in_=ygw[pw][:]), 16)
                        w8(sync, "qst")
                    block.sync(_s_sty)
                    store_done.append(cnt["qst"])

            def _g_agy(gpsimd):
                w8(gpsimd, "qst")
                cc = gpsimd.collective_compute("AllGather", ALU.bypass, ins=[agY_in],
                                               outs=[agY_out], replica_groups=[list(range(R))])
                inc("sG", cc)
                w8(gpsimd, "sG")
            block.gpsimd(_g_agy)

            def _s_cyg(sync):
                w8(sync, "sG")
                k_add = [0]
                for arm in sync.switch_core_id(128):
                    ml = arm.logical % 8
                    n = 0
                    for kk in range(R):
                        a = sync.dma_start(out=c_yg[128 * kk:128 * (kk + 1), :],
                                           in_=agY_out[kk, :, TL * ml:TL * (ml + 1)])
                        a.then_inc(sem["qld"], 16); n += 16
                    k_add[0] = n
                cnt["qld"] += k_add[0]
            block.sync(_s_cyg)

        if stages >= 3:
            with contextlib.ExitStack() as P3S:
                def sb3(n, s, d):
                    return P3S.enter_context(nc.sbuf_tensor(n, s, d))
                wdb = [sb3(f"wdb{i}", [128, HC, 128], BF16) for i in range(2)]
                ht = sb3("ht", [128, HC, PT], BF16)
                gt = sb3("gt", [128, HC, PT], BF16)
                ygt = sb3("ygt", [128, DC, PT], BF16)
                silb = sb3("silb", [128, PT], F32)

                for k in range(NT1):
                    c0 = PT * k
                    cs = slice(c0, c0 + PT)

                    def _s_ld3(sync, k=k, cs=cs):
                        w8(sync, "gate", cnt["gate"])
                        w8(sync, "qld")
                        inc("qld", sync.dma_start(
                            out=ygt[:], in_=c_yg.rearrange("(c p) t -> p c t", p=128)[:, :, cs]), 16)
                        inc("qld", sync.dma_start(
                            out=GB[:, :, 0:PT], in_=c_xn.rearrange("(c p) t -> p c t", p=128)[:, :, cs]), 16)
                        inc("qld", sync.dma_start(
                            out=GC[:, :, 0:PT], in_=c_x.rearrange("(c p) t -> p c t", p=128)[:, :, cs]), 16)
                        inc("qld", sync.dma_start(
                            out=xnbft[:], in_=c_xnbf.rearrange("(c p) t -> p c t", p=128)[:, :, cs]), 16)
                        inc("qld", sync.dma_start(
                            out=wdb[0][:].rearrange("p (kc mc) m -> p kc mc m", kc=4),
                            in_=agY_out[0:4, :, T:].rearrange("kc p (mc m) -> p kc mc m", m=128)), 16)
                        inc("qld", sync.dma_start(
                            out=wdb[1][:].rearrange("p (kc mc) m -> p kc mc m", kc=4),
                            in_=agY_out[4:8, :, T:].rearrange("kc p (mc m) -> p kc mc m", m=128)), 16)
                        w8(sync, "qld")
                    if p3_sub >= 1:
                        block.sync(_s_ld3)

                    t3 = cnt["sT"]
                    fb3 = cnt["fB"]
                    NMC = 8

                    def _t_op(tensor, t3=t3, fb3=fb3):
                        w8(tensor, "qld")
                        for mc in range(NMC):
                            if mc >= 4:
                                tensor.wait_ge(sem["fB"], fb3 + mc - 3)
                            for kc in range(DC):
                                wsl = wdb[kc // 4][:, (kc % 4) * 8 + mc, :]
                                tensor.matmul(pb[mc % 4][:, 0:512], wsl, ygt[:, kc, :],
                                              start=(kc == 0), stop=(kc == DC - 1))
                            inc("sT", tensor.drain())
                    if p3_sub >= 2:
                        block.tensor(_t_op)

                    def _v_ope(vector, t3=t3):
                        for mc in range(NMC):
                            vector.wait_ge(sem["sT"], t3 + mc + 1)
                            inc("fB", vector.scalar_tensor_tensor(
                                GA[:, mc, 0:512], pb[mc % 4][:, 0:512], s_outb[:, mc:mc + 1],
                                GB[:, mc, 0:PT], ALU.add, ALU.add))
                    if p3_sub >= 2:
                        block.vector(_v_ope)

                    def _c_sq3(scalar):
                        w8(scalar, "fB")
                        scalar.drain()
                        inc("sS", scalar.activation(GD[:, :, 0:PT], GA[:, :, 0:PT], AF.Square))
                    if p3_sub >= 3:
                        block.scalar(_c_sq3)

                    def _t_st3(tensor):
                        w8(tensor, "sS")
                        for c in range(DC):
                            fl, ll = (c == 0), (c == DC - 1)
                            tensor.matmul(pb[4][0:1, 0:512], ones_col[:], GA[:, c, 0:PT], start=fl, stop=ll)
                            tensor.matmul(pb[6][0:1, 0:512], ones_col[:], GD[:, c, 0:PT], start=fl, stop=ll)
                        inc("sT", tensor.drain())
                    if p3_sub >= 3:
                        block.tensor(_t_st3)

                    mean3 = stat[0:1, 0:PT]
                    ex23 = stat[0:1, EXT:EXT + PT]
                    isg3 = stat[0:1, EXT:EXT + PT]
                    nmis3 = stat[0:1, 2 * EXT:2 * EXT + PT]

                    def _v_st3(vector):
                        w8(vector, "sT")
                        vector.drain()
                        vector.tensor_scalar(mean3, pb[4][0:1, 0:512], 1.0 / D, None, ALU.mult)
                        vector.tensor_scalar(ex23, pb[6][0:1, 0:512], 1.0 / D, None, ALU.mult)
                        vector.drain()
                        vector.tensor_tensor(nmis3, mean3, mean3, ALU.mult)
                        vector.drain()
                        vector.tensor_tensor(ex23, ex23, nmis3, ALU.subtract)
                        vector.drain()
                        inc("sV", vector.tensor_scalar(ex23, ex23, 1e-5, None, ALU.add))
                    if p3_sub >= 4:
                        block.vector(_v_st3)

                    def _c_isg3(scalar):
                        w8(scalar, "sV")
                        scalar.drain()
                        inc("sS", scalar.activation(isg3, ex23, AF.Ln))
                    if p3_sub >= 4:
                        block.scalar(_c_isg3)

                    def _v_h3(vector):
                        w8(vector, "sS")
                        vector.drain()
                        inc("sV", vector.tensor_scalar(isg3, isg3, -0.5, None, ALU.mult))
                    if p3_sub >= 4:
                        block.vector(_v_h3)

                    def _c_isg3b(scalar):
                        w8(scalar, "sV")
                        scalar.drain()
                        inc("sS", scalar.activation(isg3, isg3, AF.Exp))
                    if p3_sub >= 4:
                        block.scalar(_c_isg3b)

                    def _v_nm3(vector):
                        w8(vector, "sS")
                        vector.drain()
                        inc("sV", vector.scalar_tensor_tensor(nmis3, mean3, -1.0, isg3,
                                                              ALU.mult, ALU.mult))
                    if p3_sub >= 4:
                        block.vector(_v_nm3)

                    def _t_bc3(tensor):
                        w8(tensor, "sV")
                        tensor.matmul(pb[5][:, 0:512], ones_row[:], isg3, start=True, stop=True)
                        tensor.matmul(pb[7][:, 0:512], ones_row[:], nmis3, start=True, stop=True)
                        inc("sT", tensor.drain())
                    if p3_sub >= 4:
                        block.tensor(_t_bc3)

                    def _v_ln2(vector):
                        w8(vector, "sT")
                        vector.drain()
                        vector.tensor_tensor(GE[:, :, 0:PT], GA[:, :, 0:PT], bcd(pb[5][:, 0:512]), ALU.mult)
                        vector.drain()
                        inc("sV", vector.tensor_tensor(GE[:, :, 0:PT], GE[:, :, 0:PT],
                                                       bcd(pb[7][:, 0:512]), ALU.add))
                    if p3_sub >= 5:
                        block.vector(_v_ln2)

                    def _v_aff(vector):
                        vector.drain()
                        for c in range(DC):
                            vector.tensor_scalar(GD[:, c, 0:PT], GE[:, c, 0:PT],
                                                 s_g[:, c:c + 1], s_beta[:, c:c + 1],
                                                 ALU.mult, ALU.add)
                        vector.drain()
                        inc("sV", vector.tensor_tensor(GF[:, :, 0:PT], GC[:, :, 0:PT],
                                                       GD[:, :, 0:PT], ALU.add))
                    if p3_sub >= 5:
                        block.vector(_v_aff)

                    fc_base = cnt["fC"]
                    qld_base = cnt["qld"]

                    def _s_w12b(sync, fc_base=fc_base):
                        w8(sync, "sT")
                        for mc in range(HC):
                            if mc >= 2:
                                sync.wait_ge(sem["fC"], fc_base + mc - 1)
                            inc("qld", sync.dma_start(
                                out=wdb[mc % 2][:, 0:8, :],
                                in_=w1WT[mc].rearrange("p (kc m) -> p kc m", m=128)), 16)
                            inc("qld", sync.dma_start(
                                out=wdb[mc % 2][:, 8:16, :],
                                in_=w2WT[mc].rearrange("p (kc m) -> p kc m", m=128)), 16)
                            w8(sync, "qld")
                    if p3_sub >= 6:
                        block.sync(_s_w12b)

                    t4 = cnt["sT"]

                    def _t_w12(tensor, qld_base=qld_base, fc_base=fc_base):
                        for mc in range(HC):
                            tensor.wait_ge(sem["qld"], qld_base + 32 * (mc + 1))
                            if mc >= 2:
                                tensor.wait_ge(sem["fC"], fc_base + mc - 1)
                            ba = 2 * (mc % 2)
                            for kc in range(DC):
                                tensor.matmul(pb[ba][:, 0:512], wdb[mc % 2][:, kc, :],
                                              xnbft[:, kc, :], start=(kc == 0), stop=(kc == DC - 1))
                            inc("sT", tensor.drain())
                            for kc in range(DC):
                                tensor.matmul(pb[ba + 1][:, 0:512], wdb[mc % 2][:, 8 + kc, :],
                                              xnbft[:, kc, :], start=(kc == 0), stop=(kc == DC - 1))
                            inc("sT", tensor.drain())
                    if p3_sub >= 6:
                        block.tensor(_t_w12)

                    s4 = cnt["sS"]
                    fcw = cnt["fC"]

                    def _c_w1(scalar, t4=t4, fcw=fcw):
                        for mc in range(HC):
                            scalar.wait_ge(sem["sT"], t4 + 2 * mc + 1)
                            if mc >= 1:
                                scalar.wait_ge(sem["fC"], fcw + mc)
                            scalar.drain()
                            inc("sS", scalar.activation(silb[:], pb[2 * (mc % 2)][:, 0:512], AF.Silu,
                                                        scale=s_w1s[:, mc:mc + 1]))
                    if p3_sub >= 6:
                        block.scalar(_c_w1)

                    def _v_w2(vector, t4=t4, s4=s4):
                        for mc in range(HC):
                            vector.wait_ge(sem["sT"], t4 + 2 * mc + 2)
                            vector.wait_ge(sem["sS"], s4 + mc + 1)
                            inc("fC", vector.scalar_tensor_tensor(
                                ht[:, mc, :], pb[2 * (mc % 2) + 1][:, 0:512], s_w2s[:, mc:mc + 1],
                                silb[:], ALU.mult, ALU.mult))
                    if p3_sub >= 6:
                        block.vector(_v_w2)

                    def ffn_big(name, wsrc, rhs_buf, out_buf, s_scale, fc_name="fC"):
                        fcb = cnt[fc_name]
                        qb = cnt["qld"]

                        tprev = cnt["sT"]

                        def _s_w(sync):
                            sync.wait_ge(sem["sT"], tprev)
                            for mc in range(HC):
                                if mc >= 2:
                                    sync.wait_ge(sem[fc_name], fcb + mc - 1)
                                inc("qld", sync.dma_start(
                                    out=wdb[mc % 2][:],
                                    in_=wsrc[mc].rearrange("p (kc m) -> p kc m", m=128)), 16)
                                w8(sync, "qld")
                        block.sync(_s_w)

                        tb = cnt["sT"]

                        def _t_w(tensor):
                            for mc in range(HC):
                                tensor.wait_ge(sem["qld"], qb + 16 * (mc + 1))
                                if mc < 2:
                                    tensor.wait_ge(sem[fc_name], fcb)
                                else:
                                    tensor.wait_ge(sem[fc_name], fcb + mc - 1)
                                for kc in range(HC):
                                    tensor.matmul(pb[mc % 2][:, 0:512], wdb[mc % 2][:, kc, :],
                                                  rhs_buf[:, kc, :], start=(kc == 0), stop=(kc == HC - 1))
                                inc("sT", tensor.drain())
                        block.tensor(_t_w)

                        def _c_e(scalar):
                            scalar.drain()
                            for mc in range(HC):
                                scalar.wait_ge(sem["sT"], tb + mc + 1)
                                inc(fc_name, scalar.activation(out_buf[:, mc, :], pb[mc % 2][:, 0:512],
                                                               AF.Gelu, scale=s_scale[:, mc:mc + 1]))
                        block.scalar(_c_e)

                    if p3_sub >= 7:
                        ffn_big("h1", h1WT, ht, gt, s_h1s)
                    if p3_sub >= 8:
                        ffn_big("h2", h2WT, gt, ht, s_h2s)

                    fcb3 = cnt["fC"]
                    qb3 = cnt["qld"]

                    tprev3 = cnt["sT"]

                    def _s_w3(sync):
                        sync.wait_ge(sem["sT"], tprev3)
                        for mc in range(DC):
                            if mc >= 2:
                                sync.wait_ge(sem["fC"], fcb3 + mc - 1)
                            inc("qld", sync.dma_start(
                                out=wdb[mc % 2][:],
                                in_=w3WT[mc].rearrange("p (kc m) -> p kc m", m=128)), 16)
                            w8(sync, "qld")
                    if p3_sub >= 8:
                        block.sync(_s_w3)

                    tb3 = cnt["sT"]

                    def _t_w3(tensor):
                        for mc in range(DC):
                            tensor.wait_ge(sem["qld"], qb3 + 16 * (mc + 1))
                            if mc < 2:
                                tensor.wait_ge(sem["fC"], fcb3)
                            else:
                                tensor.wait_ge(sem["fC"], fcb3 + mc - 1)
                            for kc in range(HC):
                                tensor.matmul(pb[mc % 2][:, 0:512], wdb[mc % 2][:, kc, :],
                                              ht[:, kc, :], start=(kc == 0), stop=(kc == HC - 1))
                            inc("sT", tensor.drain())
                    if p3_sub >= 8:
                        block.tensor(_t_w3)

                    def _v_w3(vector, tb3=tb3):
                        vector.drain()
                        for mc in range(DC):
                            vector.wait_ge(sem["sT"], tb3 + mc + 1)
                            inc("fC", vector.scalar_tensor_tensor(
                                GA[:, mc, 0:PT], pb[mc % 2][:, 0:512], s_w3s[:, mc:mc + 1],
                                GF[:, mc, 0:PT], ALU.mult, ALU.add))
                    if p3_sub >= 8:
                        block.vector(_v_w3)

                    def _s_out(sync, cs=cs):
                        w8(sync, "fC")
                        inc("qst", sync.dma_start(
                            out=out_fm.rearrange("(c p) t -> p c t", p=128)[:, :, cs],
                            in_=GA[:, :, 0:PT]), 16)
                        w8(sync, "qst")
                    if p3_sub >= 9:
                        block.sync(_s_out)

                    def _v_g3(vector):
                        w8(vector, "qst")
                        inc("gate", vector.drain())
                    if p3_sub >= 1:
                        block.vector(_v_g3)

        def _s_fin(sync):
            w8(sync, "gate")
            w8(sync, "sV")
            inc("qst", sync.dma_start(out=oid, in_=mk[:]), 16)
            for d_out, d_src in dbg:
                inc("qst", sync.dma_start(out=d_out, in_=d_src), 16)
            w8(sync, "qst")
        block.sync(_s_fin)

    return nc


def _wlay(w):
    WT = np.ascontiguousarray(np.asarray(w, np.float32).T)
    K, M = WT.shape
    kc, mcn = K // 128, M // 128
    return np.ascontiguousarray(
        WT.reshape(kc, 128, mcn, 128).transpose(2, 1, 0, 3).reshape(mcn, 128, kc * 128)
    ).astype(ml_dtypes.bfloat16)


def prep_inputs(inputs):
    f32 = np.float32
    bf = ml_dtypes.bfloat16
    x = np.asarray(inputs["L0_attn_x"], f32)[0]
    xT = np.ascontiguousarray(x.T)
    shared = {
        "mtab": np.arange(8, dtype=f32).reshape(8, 1),
        "inWT": np.ascontiguousarray(np.asarray(inputs["L0_attn_in_proj_w"], f32).T).astype(bf),
        "in_b_pack": np.ascontiguousarray(np.asarray(inputs["L0_attn_in_proj_b"], f32).reshape(16, 128).T),
        "cw_pack": np.ascontiguousarray(
            np.asarray(inputs["L0_attn_conv_w"], f32)[:, 0, :].reshape(8, 128, 4).transpose(1, 0, 2).reshape(128, 32)),
        "cb_pack": np.ascontiguousarray(np.asarray(inputs["L0_attn_conv_b"], f32).reshape(8, 128).T),
        "A_log_full": np.ascontiguousarray(np.asarray(inputs["L0_attn_A_log"], f32)),
        "bcWT": np.ascontiguousarray(
            np.concatenate([np.asarray(inputs["L0_attn_B_w"], f32),
                            np.asarray(inputs["L0_attn_C_w"], f32)], 0).T).astype(bf),
        "bcb": np.concatenate([np.asarray(inputs["L0_attn_B_b"], f32),
                               np.asarray(inputs["L0_attn_C_b"], f32)]).reshape(64, 1),
        "dtWT": np.ascontiguousarray(np.asarray(inputs["L0_attn_dt_w"], f32).T).astype(bf),
        "dtb_pack": np.ascontiguousarray(np.asarray(inputs["L0_attn_dt_b"], f32).reshape(8, 128).T),
        "outWT": np.ascontiguousarray(np.asarray(inputs["L0_attn_out_w"], f32).T).astype(bf),
        "outb_pack": np.ascontiguousarray(np.asarray(inputs["L0_attn_out_b"], f32).reshape(8, 128).T),
        "g_pack": np.ascontiguousarray(np.asarray(inputs["L0_attn_ln_g"], f32).reshape(8, 128).T),
        "beta_pack": np.ascontiguousarray(np.asarray(inputs["L0_attn_ln_beta"], f32).reshape(8, 128).T),
        "w1WT": _wlay(inputs["L0_ffn_w1_w"]),
        "w2WT": _wlay(inputs["L0_ffn_w2_w"]),
        "h1WT": _wlay(inputs["L0_ffn_h1_w"]),
        "h2WT": _wlay(inputs["L0_ffn_h2_w"]),
        "w3WT": _wlay(inputs["L0_ffn_w3_w"]),
        "w1s_pack": np.ascontiguousarray(np.asarray(inputs["L0_ffn_w1_s"], f32).reshape(32, 128).T),
        "w2s_pack": np.ascontiguousarray(np.asarray(inputs["L0_ffn_w2_s"], f32).reshape(32, 128).T),
        "h1s_pack": np.ascontiguousarray(np.asarray(inputs["L0_ffn_h1_s"], f32).reshape(32, 128).T),
        "h2s_pack": np.ascontiguousarray(np.asarray(inputs["L0_ffn_h2_s"], f32).reshape(32, 128).T),
        "w3s_pack": np.ascontiguousarray(np.asarray(inputs["L0_ffn_w3_s"], f32).reshape(8, 128).T),
        "oh64": np.eye(64, dtype=f32),
    }
    in_maps = []
    for l in range(R):
        xe = np.zeros((D, TL + 4), f32)
        g0 = TL * l - 2
        lo = max(0, g0)
        hi = min(T, g0 + TL + 4)
        xe[:, lo - g0:hi - g0] = xT[:, lo:hi]
        m3 = np.ones((128, 3), f32)
        if l == 0:
            m3[:, 0:2] = 0.0
        if l == R - 1:
            m3[:, 2] = 0.0
        im = {"x_ext": xe, "mask3": m3}
        im.update(shared)
        in_maps.append(im)
    return in_maps


def kernel(**inputs):
    in_maps = prep_inputs(inputs)
    nc = build()
    res = run_bass_kernel_spmd(nc, in_maps, core_ids=list(range(R)))
    outT = np.zeros((D, T), np.float32)
    seen = []
    for l in range(R):
        ml = int(round(float(res.results[l]["oid"][0, 0])))
        seen.append(ml)
        outT[:, TL * ml:TL * (ml + 1)] = res.results[l]["out_fm"]
    assert sorted(seen) == list(range(R)), f"slice ids not a permutation: {seen}"
    return np.ascontiguousarray(outT.T)[None]
